# version 05
# baseline (speedup 1.0000x reference)
import contextlib

import numpy as np
import ml_dtypes

import concourse.bass as bass
import concourse.mybir as mybir
from concourse.bass_utils import run_bass_kernel_spmd

EPS = 1e-5
F32 = mybir.dt.float32
BF16 = mybir.dt.bfloat16
I32 = mybir.dt.int32
AF = mybir.ActivationFunctionType
ALU = mybir.AluOpType
NBF = ml_dtypes.bfloat16

T = 1024
KC = 32
MC = 67
NCH = 8
ENGINES = ("sync", "scalar", "vector", "tensor", "gpsimd")
RG = [[0, 1], [2, 3], [4, 5], [6, 7]]

_NC_CACHE = {}


class _StopPlan(Exception):
    pass


def build_nc(stop_after=None, debug=False):
    nc = bass.Bass("TRN2", target_bir_lowering=False)

    xin = nc.dram_tensor("xin", [4096, T], F32, kind="ExternalInput").ap()
    w_in = nc.dram_tensor("w_in", [MC, 128, KC, 128], F32, kind="ExternalInput").ap()
    w_out = nc.dram_tensor("w_out", [KC, 128, KC, 128], F32, kind="ExternalInput").ap()
    qexp = nc.dram_tensor("qexp", [KC, 128, KC, 128], I32, kind="ExternalInput").ap()
    convd = nc.dram_tensor("convd", [34, 128, 4, 128], BF16, kind="ExternalInput").ap()
    c_id_d = nc.dram_tensor("c_id", [128, 128], F32, kind="ExternalInput").ap()
    c_tri_d = nc.dram_tensor("c_tri", [128, 128], F32, kind="ExternalInput").ap()
    c_ind_d = nc.dram_tensor("c_ind", [64, 4096], BF16, kind="ExternalInput").ap()
    c_mup_d = nc.dram_tensor("c_mup", [128, MC], F32, kind="ExternalInput").ap()
    c_sc_d = nc.dram_tensor("c_sc", [128, KC], F32, kind="ExternalInput").ap()
    c_nszp_d = nc.dram_tensor("c_nszp", [128, KC], F32, kind="ExternalInput").ap()
    c_fb_d = nc.dram_tensor("c_fb", [128, KC], F32, kind="ExternalInput").ap()
    c_df_d = nc.dram_tensor("c_df", [1, 4096], BF16, kind="ExternalInput").ap()
    c_nw_d = nc.dram_tensor("c_nw", [128, KC], F32, kind="ExternalInput").ap()
    c_lnw_d = nc.dram_tensor("c_lnw", [128, KC], F32, kind="ExternalInput").ap()
    c_cw_d = nc.dram_tensor("c_cw", [128, 136], F32, kind="ExternalInput").ap()
    c_sh_d = nc.dram_tensor("c_sh", [128, 1], I32, kind="ExternalInput").ap()
    c_dtb_d = nc.dram_tensor("c_dtb", [64, 1], F32, kind="ExternalInput").ap()
    c_al_d = nc.dram_tensor("c_al", [64, 1], F32, kind="ExternalInput").ap()
    c_sel_d = nc.dram_tensor("c_sel", [128, 1], F32, kind="ExternalInput").ap()
    c_cv_d = nc.dram_tensor("c_cv", [1, 8], F32, kind="ExternalInput").ap()
    out_fm = nc.dram_tensor("out_fm", [4096, T], F32, kind="ExternalOutput").ap()
    if debug:
        d_xs = nc.dram_tensor("d_xs", [4096, T], BF16, kind="ExternalOutput").ap()
        d_z = nc.dram_tensor("d_z", [4096, T], BF16, kind="ExternalOutput").ap()
        d_ffn = nc.dram_tensor("d_ffn", [4096, T], F32, kind="ExternalOutput").ap()
        d_ygw = nc.dram_tensor("d_ygw", [T, 4096], BF16, kind="ExternalOutput").ap()
        d_H = nc.dram_tensor("d_H", [128, 4096], F32, kind="ExternalOutput").ap()
        d_Hg = nc.dram_tensor("d_Hg", [256, 4096], F32, kind="ExternalOutput").ap()
        d_dt = nc.dram_tensor("d_dt", [64, T], F32, kind="ExternalOutput").ap()
        d_s = nc.dram_tensor("d_s", [64, T], F32, kind="ExternalOutput").ap()
        d_sm = nc.dram_tensor("d_sm", [64, 64], F32, kind="ExternalOutput").ap()
        d_ds = nc.dram_tensor("d_ds", [64, T], F32, kind="ExternalOutput").ap()
        d_ld = nc.dram_tensor("d_ld", [64, T], F32, kind="ExternalOutput").ap()

    z_plain = nc.dram_tensor("z_plain", [4096, T], BF16).ap()
    xs_plain = nc.dram_tensor("xs_plain", [4096, T], BF16).ap()
    ffn_dr = nc.dram_tensor("ffn_dr", [4096, T], F32).ap()
    ygw_dr = nc.dram_tensor("ygw_dr", [T, 4096], BF16).ap()
    rows_d = nc.dram_tensor("rows_d", [1, 8192], F32).ap()
    srow_d = nc.dram_tensor("srow_d", [1, 4096], F32).ap()
    cc1i = nc.dram_tensor("cc1i", [128, 102], BF16)
    cc1o = nc.dram_tensor("cc1o", [256, 102], BF16)
    cc2i = nc.dram_tensor("cc2i", [128, 4096], F32)
    cc2o = nc.dram_tensor("cc2o", [256, 4096], F32)

    with contextlib.ExitStack() as es:
        En = es.enter_context

        AB1 = En(nc.sbuf_tensor("AB1", [128, 32768], BF16))
        AB2 = En(nc.sbuf_tensor("AB2", [128, 24576], BF16))
        AB3 = En(nc.sbuf_tensor("AB3", [128, 8192], BF16))
        AB4 = En(nc.sbuf_tensor("AB4", [128, 12288], BF16))
        AFR = En(nc.sbuf_tensor("AFR", [128, 7168], F32))
        c_id = En(nc.sbuf_tensor("s_id", [128, 128], F32))
        c_tri = En(nc.sbuf_tensor("s_tri", [128, 128], F32))
        c_mup = En(nc.sbuf_tensor("s_mup", [128, MC], F32))
        c_sc = En(nc.sbuf_tensor("s_sc", [128, KC], F32))
        c_nszp = En(nc.sbuf_tensor("s_nszp", [128, KC], F32))
        c_fb = En(nc.sbuf_tensor("s_fb", [128, KC], F32))
        Db = En(nc.sbuf_tensor("s_Db", [128, 4096], BF16))
        c_nw = En(nc.sbuf_tensor("s_nw", [128, KC], F32))
        c_lnw = En(nc.sbuf_tensor("s_lnw", [128, KC], F32))
        c_cw = En(nc.sbuf_tensor("s_cw", [128, 136], F32))
        c_sh = En(nc.sbuf_tensor("s_sh", [128, 1], I32))
        c_dtb = En(nc.sbuf_tensor("s_dtb", [64, 1], F32))
        c_al = En(nc.sbuf_tensor("s_al", [64, 1], F32))
        c_sel = En(nc.sbuf_tensor("s_sel", [128, 1], F32))
        c_ones = En(nc.sbuf_tensor("s_ones", [128, 1], BF16))
        c_o128 = En(nc.sbuf_tensor("s_o128", [128, 128], BF16))
        c_ones64 = En(nc.sbuf_tensor("s_ones64", [64, 128], F32))
        z64 = En(nc.sbuf_tensor("s_z64", [64, 128], F32))
        c_cv = En(nc.sbuf_tensor("s_cv", [128, 8], F32))

        psA = En(nc.psum_tensor("psA", [128, 1024], F32))
        psB = En(nc.psum_tensor("psB", [128, 1024], F32))
        psY = En(nc.psum_tensor("psY", [128, 2048], F32))
        psAv64 = [psA[:, 0:64], psB[:, 0:64]]

        sem = {}
        for s in ("dS", "dG", "sT", "sV", "sA", "sP", "sC"):
            sem[s] = En(nc.semaphore(s))

        AB1f = AB1[:, :].bitcast(F32)
        xn = AB1
        srow = AB1f[:, 0:4096]
        H_f = AB1f[:, 4096:8192]
        y_half = AB1f[:, 8192:10240]
        Htmp = AB1f[:, 10240:11264]
        ind_sb = AB3[0:64, 0:4096]

        AB2i = AB2[:, :].bitcast(I32)
        AB2f = AB2[:, :].bitcast(F32)
        wf32 = [AB2f[:, 0:4096], AB2f[:, 4096:8192]]
        qx = [AB2i[:, 0:4096], AB2i[:, 4096:8192]]
        E_sb = AB2[:, 0:4096]
        XW = AB2[:, 4096:8192]
        XD = XW
        WD = XW
        Hbf = AB2[:, 8192:12288]
        xst = AB2[:, 12288:16384]
        yg = AB2[:, 16384:20480]
        szt = AB2[:, 20480:24576]

        wb = [AB3[:, 0:4096], AB3[:, 4096:8192]]
        vb = wb

        B_fm = AB4[:, 0:1024]
        C_fm = AB4[:, 1024:2048]
        B_tok = AB4[:, 2048:3072]
        CBm = AB4[:, 3072:4096]
        dt_tok = AB4[:, 4096:4608]
        xb2 = [AB4[:, 4608:5632], AB4[:, 5632:6656]]
        zst = xb2
        sq2 = [AB4[:, 6656:7680], AB4[:, 7680:8704]]
        xsst = sq2
        xbc2 = [AB4[:, 8704:9728], AB4[:, 9728:10752]]
        dg2 = [AB4[:, 10752:11264], AB4[:, 11264:11776]]
        edge = AB4[:, 11776:11878]
        sendb = AB4[:, 11878:11980]
        xpe = AB4[:, 11980:11988]
        lamT = AB4[0:64, 11988:12116]
        gth = AB4[:, 12116:12218]
        wdt_b = AB4[:, 12218:12282]

        x2 = [AFR[:, 0:1024], AFR[:, 1024:2048]]
        b_rstd = AFR[:, 2048:3072]
        b_mr = AFR[:, 3072:4096]
        ffn2 = [AFR[:, 2048:3072], AFR[:, 3072:4096]]
        r_mean = AFR[0:1, 4096:5120]
        r_tmp = AFR[0:1, 5120:6144]
        r_rstd = AFR[0:1, 6144:7168]
        r_mr = r_mean
        r_rs1 = AFR[0:1, 4096:5120]
        r_cols = AFR[0:1, 6144:7168]
        b_stot = AFR[:, 4096:4608]
        b_seg = AFR[:, 5632:6144]
        b_send = AFR[:, 4608:4672]
        d1 = AFR[:, 4672:4736]
        we_t = AFR[:, 4736:4800]
        es_t = AFR[:, 4800:4864]
        rs_col = AFR[:, 4864:4865]
        ms_col = AFR[:, 4872:4873]
        o2 = [AFR[:, 4096:5120], AFR[:, 5120:6144]]
        b_rs1 = AFR[:, 6144:7168]
        dt_fm = AFR[0:64, 6144:7168]
        s_all = AB1f[0:64, 11264:12288]
        dtv = AB1f[0:64, 12288:13312]
        dt_soft = dtv
        t1 = AB1f[0:64, 13312:14336]
        logdA = t1
        s_tok = AB1f[:, 14336:14848]
        stot = AB1f[0:64, 14848:14856]
        seg = AB1f[0:64, 14864:14872]
        lam_a = AB1f[0:64, 14880:14888]
        nA = AB1f[0:64, 14896:14897]
        t2d = AB1f[0:64, 15360:16384]
        b_cols = AFR[:, 5120:6144]

        prog = {k: [] for k in ENGINES}
        PLAN = {k: [] for k in ENGINES}
        nc._plan = PLAN
        C = {s: 0 for s in sem}
        esem = {"tensor": "sT", "vector": "sV", "scalar": "sA", "gpsimd": "sP"}

        def _w(eng, waits):
            for n, v in waits:
                if v > 0:
                    PLAN[eng].append(("w", n, v))
                    prog[eng].append(lambda e, s=sem[n], vv=v: e.wait_ge(s, vv))

        def dma(q, out, in_, wait=(), **kw):
            _w(q, wait)
            sn = "dS" if q == "sync" else "dG"
            C[sn] += 16
            PLAN[q].append(("i", sn, 16))
            prog[q].append(
                lambda e, o=out, i=in_, k2=dict(kw), s=sem[sn]:
                e.dma_start(out=o, in_=i, **k2).then_inc(s, 16))
            return C[sn]

        def dmat(q, out, in_, wait=()):
            _w(q, wait)
            sn = "dS" if q == "sync" else "dG"
            C[sn] += 16
            PLAN[q].append(("i", sn, 16))
            prog[q].append(
                lambda e, o=out, i=in_, s=sem[sn]:
                e.dma_start_transpose(o, i).then_inc(s, 16))
            return C[sn]

        def op(eng, builder, wait=(), inc=True, selfwait=False):
            if selfwait:
                wait = list(wait) + [(esem[eng], C[esem[eng]])]
            _w(eng, wait)
            sn = esem[eng]
            if inc:
                C[sn] += 1
                PLAN[eng].append(("i", sn, 1))
                prog[eng].append(lambda e, b=builder, s=sem[sn]: b(e).then_inc(s, 1))
                return C[sn]
            PLAN[eng].append(("i", None, 0))
            prog[eng].append(lambda e, b=builder: b(e))
            return C[sn]

        def tmark(wait=()):
            return op("tensor", lambda e: e.nop(), wait=wait)

        def mm(out, lhsT, rhs, start, stop, wait=(), inc=False):
            return op("tensor",
                      lambda e, o=out, l=lhsT, r=rhs, st=start, sp=stop:
                      e.matmul(o, l, r, start=st, stop=sp, skip_group_check=True),
                      wait=wait, inc=inc)

        def coll(kind, ins, outs, wait=()):
            _w("gpsimd", wait)
            C["sC"] += 1
            PLAN["gpsimd"].append(("i", "sC", 1))
            prog["gpsimd"].append(
                lambda e, k=kind, i=ins, o=outs, s=sem["sC"]:
                e.collective_compute(k, ALU.bypass, replica_groups=RG, ins=i, outs=o)
                .then_inc(s, 1))
            return C["sC"]

        def barrier():
            snap = [(n, C[n]) for n in C]
            for eng in ENGINES:
                _w(eng, snap)

        try:
            for dst, src in [(c_id, c_id_d), (c_tri, c_tri_d), (c_mup, c_mup_d),
                             (c_sc, c_sc_d), (c_nszp, c_nszp_d), (c_fb, c_fb_d),
                             (c_nw, c_nw_d), (c_lnw, c_lnw_d),
                             (c_cw, c_cw_d), (c_sh, c_sh_d), (c_dtb, c_dtb_d),
                             (c_al, c_al_d), (c_sel, c_sel_d)]:
                dma("gpsimd", dst[:, :], src)
            dma("sync", c_cv[:, :], c_cv_d.partition_broadcast(128))
            dma("sync", Db[:, :], c_df_d.partition_broadcast(128))
            for _i, _v in enumerate([EPS, -0.5, -1.0, 1.0, 1.0 / 4096]):
                nc.const_aps.register(F32, _v, c_cv[:, _i:_i + 1])
            op("vector", lambda e: e.memset(c_ones[:, :], 1.0))
            op("vector", lambda e: e.memset(c_o128[:, :], 1.0))
            op("vector", lambda e: e.memset(c_ones64[:, :], 1.0))
            op("vector", lambda e: e.memset(z64[:, :], 0.0))
            barrier()

            if stop_after is not None and stop_after <= 0:
                raise _StopPlan()
            tdone = [0, 0]
            for k in range(KC):
                b = k % 2
                ld = dma("sync", x2[b], xin[k * 128:(k + 1) * 128, :],
                         wait=[("sT", tdone[b])])
                cv = op("vector", lambda e, b=b: e.tensor_copy(xb2[b][:, :], x2[b]),
                        wait=[("dS", ld), ("sT", tdone[b])])
                sv = op("scalar", lambda e, b=b: e.activation(sq2[b][:, :], x2[b], AF.Square),
                        wait=[("dS", ld), ("sT", tdone[b])])
                for h in range(2):
                    mm(psA[:, h * 512:(h + 1) * 512], c_o128[:, :],
                       xb2[b][:, h * 512:(h + 1) * 512], start=(k == 0), stop=(k == KC - 1),
                       wait=([("sV", cv), ("sA", sv)] if h == 0 else []))
                    tdone[b] = mm(psB[:, h * 512:(h + 1) * 512], c_o128[:, :],
                       sq2[b][:, h * 512:(h + 1) * 512], start=(k == 0), stop=(k == KC - 1),
                       inc=(h == 1))
            tstat = tdone[(KC - 1) % 2]
            v = op("vector", lambda e: e.tensor_copy(b_mr, psA[:, :]),
                   wait=[("sT", tstat)])
            v = op("vector", lambda e: e.tensor_copy(b_rstd, psB[:, :]))
            v = op("vector", lambda e: e.tensor_scalar(out=b_mr, in0=b_mr,
                    scalar1=1.0 / 4096, scalar2=None, op0=ALU.mult))
            v = op("vector", lambda e: e.tensor_scalar(out=b_rstd, in0=b_rstd,
                    scalar1=1.0 / 4096, scalar2=None, op0=ALU.mult))
            v = op("vector", lambda e: e.tensor_mul(x2[0][:, :], b_mr, b_mr))
            v = op("vector", lambda e: e.tensor_sub(b_rstd, b_rstd, x2[0][:, :]))
            a = op("scalar", lambda e: e.activation(b_rstd, b_rstd, AF.Ln, bias=EPS),
                   wait=[("sV", v)])
            a = op("scalar", lambda e: e.activation(b_rstd, b_rstd, AF.Exp, scale=-0.5),
                   selfwait=True)
            v = op("vector", lambda e: e.tensor_mul(b_mr, b_mr, b_rstd), wait=[("sA", a)])
            barrier()

            if stop_after is not None and stop_after <= 1:
                raise _StopPlan()
            tdone = [0, 0]
            for k in range(KC):
                b = k % 2
                ld = dma("sync", x2[b], xin[k * 128:(k + 1) * 128, :],
                         wait=[("sT", tdone[b])])
                v = op("vector", lambda e, b=b: e.tensor_mul(x2[b], x2[b], b_rstd),
                       wait=[("dS", ld)])
                v = op("vector", lambda e, b=b, k=k:
                       e.tensor_sub(xn[:, k * 1024:(k + 1) * 1024], x2[b], b_mr))
                sv = op("scalar", lambda e, b=b, k=k: e.activation(
                    sq2[b][:, :], xn[:, k * 1024:(k + 1) * 1024], AF.Square),
                    wait=[("sV", v), ("sT", tdone[b])])
                for h in range(2):
                    mm(psY[:, h * 512:(h + 1) * 512], c_o128[:, :],
                       sq2[b][:, h * 512:(h + 1) * 512], start=(k == 0), stop=(k == KC - 1),
                       wait=[("sA", sv)] if h == 0 else [])
                    tdone[b] = mm(psY[:, 1024 + h * 512:1024 + (h + 1) * 512], c_o128[:, :],
                       xn[:, k * 1024 + h * 512:k * 1024 + (h + 1) * 512],
                       start=(k == 0), stop=(k == KC - 1),
                       wait=([("sV", v)] if h == 0 else []), inc=(h == 1))
            tstat = tdone[(KC - 1) % 2]
            v2 = op("vector", lambda e: e.tensor_copy(b_rs1, psY[:, 0:1024]),
                    wait=[("sT", tstat)])
            a = op("scalar", lambda e: e.activation(b_rs1, b_rs1, AF.Ln,
                    bias=EPS, scale=1.0 / 4096), wait=[("sV", v2)])
            a = op("scalar", lambda e: e.activation(b_rs1, b_rs1, AF.Exp, scale=-0.5),
                   selfwait=True)
            v = op("vector", lambda e: e.tensor_copy(b_cols, psY[:, 1024:2048]),
                   wait=[("sT", tstat)])
            barrier()

            if stop_after is not None and stop_after <= 2:
                raise _StopPlan()
            qld = [0, 0]
            vext = [0, 0]
            vcp = [0, 0]
            tmm = [0, 0]
            vev = [0, 0]
            fdma = [0, 0]
            for mi in range(KC):
                b = mi % 2
                qld[b] = dma("gpsimd", qx[b], qexp[mi], wait=[("sV", vext[b])])
                vext[b] = op("vector", lambda e, b=b: e.tensor_scalar(
                    out=qx[b], in0=qx[b], scalar1=c_sh[:, 0:1], scalar2=63,
                    op0=ALU.logical_shift_right, op1=ALU.bitwise_and),
                    wait=[("dG", qld[b]), ("sV", vcp[b])])
                vcp[b] = op("vector", lambda e, b=b: e.tensor_copy(vb[b][:, :], qx[b]),
                            wait=[("sT", tmm[b])])
                ps = psA if b == 0 else psB
                for kc in range(KC):
                    for h in range(2):
                        tmm[b] = mm(ps[:, h * 512:(h + 1) * 512],
                           vb[b][:, kc * 128:(kc + 1) * 128],
                           xn[:, kc * 1024 + h * 512:kc * 1024 + (h + 1) * 512],
                           start=(kc == 0), stop=(kc == KC - 1),
                           wait=([("sV", vcp[b]), ("sV", vev[b])] if kc == 0 and h == 0 else []),
                           inc=(kc == KC - 1 and h == 1))
                v = op("vector", lambda e, ps=ps, b=b, mi=mi: e.scalar_tensor_tensor(
                    out=ffn2[b], in0=ps[:, :], scalar=c_sc[:, mi:mi + 1],
                    in1=c_fb[:, mi:mi + 1].broadcast_to([128, 1024]),
                    op0=ALU.mult, op1=ALU.add),
                    wait=[("sT", tmm[b]), ("dS", fdma[b])])
                vev[b] = op("vector", lambda e, b=b, mi=mi: e.scalar_tensor_tensor(
                    out=ffn2[b], in0=b_cols, scalar=c_nszp[:, mi:mi + 1],
                    in1=ffn2[b], op0=ALU.mult, op1=ALU.add))
                fdma[b] = dma("sync", ffn_dr[mi * 128:(mi + 1) * 128, :], ffn2[b],
                              wait=[("sV", vev[b])])
            barrier()

            if stop_after is not None and stop_after <= 3:
                raise _StopPlan()
            for k in range(KC):
                op("vector", lambda e, k=k: e.scalar_tensor_tensor(
                    out=xn[:, k * 1024:(k + 1) * 1024],
                    in0=xn[:, k * 1024:(k + 1) * 1024],
                    scalar=c_lnw[:, k:k + 1], in1=b_rs1, op0=ALU.mult, op1=ALU.mult))
            x1 = xn
            barrier()

            if stop_after is not None and stop_after <= 4:
                raise _StopPlan()
            wld = [0, 0]
            wfree = [0, 0]
            psfree = [0, 0]
            dgld = [0, 0]
            convdone = 0
            last_silu = 0
            silu_x2 = [0, 0]
            xsdma = [0, 0]
            vlast_bc = 0
            wcast = [0, 0]
            for j, mi in enumerate(range(32, 67)):
                b = j % 2
                wld[b] = dma("gpsimd", wf32[b], w_in[mi], wait=[("sV", wcast[b])])
                wcast[b] = op("vector", lambda e, b=b: e.tensor_copy(
                    wb[b][:, :], wf32[b]), wait=[("dG", wld[b]), ("sT", wfree[b])])
                ti = mi - 32
                ps = psA if b == 0 else psB
                for kc in range(KC):
                    for h in range(2):
                        wfree[b] = mm(ps[:, h * 512:(h + 1) * 512],
                           wb[b][:, kc * 128:(kc + 1) * 128],
                           x1[:, kc * 1024 + h * 512:kc * 1024 + (h + 1) * 512],
                           start=(kc == 0), stop=(kc == KC - 1),
                           wait=([("sV", wcast[b]), ("sV", psfree[b])]
                                 if kc == 0 and h == 0 else []),
                           inc=(kc == KC - 1 and h == 1))
                if mi < 66:
                    ev = op("vector", lambda e, ps=ps, b=b, mi=mi: e.tensor_tensor(
                        xbc2[b][:, :], ps[:, :],
                        c_mup[:, mi:mi + 1].broadcast_to([128, 1024]), ALU.mult),
                        wait=[("sT", wfree[b])])
                    psfree[b] = ev
                    op("vector", lambda e, b=b, ti=ti: e.tensor_copy(
                        edge[:, ti * 3:(ti + 1) * 3], xbc2[b][:, 0:3]))
                    vlast_bc = op("vector", lambda e, b=b, ti=ti: e.tensor_copy(
                        sendb[:, ti * 3:(ti + 1) * 3], xbc2[b][:, 1021:1024]))
                    dgld[b] = dma("gpsimd", dg2[b][:, :], convd[ti],
                                  wait=[("sT", convdone)])
                    for k in range(4):
                        mm(psY[:, 0:512], dg2[b][:, k * 128:(k + 1) * 128],
                           xbc2[b][:, k:k + 512], start=(k == 0), stop=(k == 3),
                           wait=[("dG", dgld[b]), ("sV", ev), ("sV", last_silu)]
                           if k == 0 else [])
                        convdone = mm(psY[:, 512:1021], dg2[b][:, k * 128:(k + 1) * 128],
                           xbc2[b][:, 512 + k:1021 + k], start=(k == 0), stop=(k == 3),
                           inc=(k == 3))
                    last_silu = op("vector", lambda e, b=b: e.tensor_copy(
                        x2[b][:, 0:1021], psY[:, 0:1021]),
                        wait=[("sT", convdone), ("sA", silu_x2[b])])
                    sv = op("scalar", lambda e, b=b: e.activation(
                        xsst[b][:, 0:1021], x2[b][:, 0:1021], AF.Silu),
                        wait=[("sV", last_silu), ("dS", xsdma[b])])
                    silu_x2[b] = sv
                    if ti < 32:
                        xsdma[b] = dma("sync", xs_plain[ti * 128:(ti + 1) * 128, 3:1024],
                                       xsst[b][:, 0:1021], wait=[("sA", sv)])
                    elif ti == 32:
                        op("vector", lambda e, b=b: e.tensor_copy(
                            B_fm[:, 3:1024], xsst[b][:, 0:1021]), wait=[("sA", sv)])
                    else:
                        op("vector", lambda e, b=b: e.tensor_copy(
                            C_fm[:, 3:1024], xsst[b][:, 0:1021]), wait=[("sA", sv)])
                else:
                    ev = op("vector", lambda e, ps=ps, b=b: e.tensor_tensor(
                        x2[b][:, :], ps[:, :],
                        c_mup[:, 66:67].broadcast_to([128, 1024]), ALU.mult),
                        wait=[("sT", wfree[b])])
                    psfree[b] = ev
                    op("vector", lambda e, b=b: e.tensor_copy(
                        dt_fm, x2[b][0:64, :]))
            d_snd = dma("sync", cc1i.ap(), sendb[:, :], wait=[("sV", vlast_bc)])
            cc1done = coll("AllGather", [cc1i.ap()], [cc1o.ap()], wait=[("dS", d_snd)])

            if stop_after is not None and stop_after <= 5:
                raise _StopPlan()
            zev = [0, 0]
            zfree = [0, 0]
            zdma = [0, 0]
            for mi in range(32):
                b = mi % 2
                wld[b] = dma("gpsimd", wf32[b], w_in[mi], wait=[("sV", wcast[b])])
                wcast[b] = op("vector", lambda e, b=b: e.tensor_copy(
                    wb[b][:, :], wf32[b]), wait=[("dG", wld[b]), ("sT", wfree[b])])
                ps = psA if b == 0 else psB
                for kc in range(KC):
                    for h in range(2):
                        wfree[b] = mm(ps[:, h * 512:(h + 1) * 512],
                           wb[b][:, kc * 128:(kc + 1) * 128],
                           x1[:, kc * 1024 + h * 512:kc * 1024 + (h + 1) * 512],
                           start=(kc == 0), stop=(kc == KC - 1),
                           wait=([("sV", wcast[b]), ("sV", psfree[b]), ("sV", zfree[b])]
                           if kc == 0 and h == 0 else []),
                           inc=(kc == KC - 1 and h == 1))
                psfree[b] = 0
                zfree[b] = op("vector", lambda e, ps=ps, b=b, mi=mi: e.tensor_tensor(
                    x2[b][:, :], ps[:, :],
                    c_mup[:, mi:mi + 1].broadcast_to([128, 1024]), ALU.mult),
                    wait=[("sT", wfree[b]), ("sA", zev[b])])
                zev[b] = op("scalar", lambda e, b=b: e.activation(
                    zst[b][:, :], x2[b][:, :], AF.Silu),
                    wait=[("sV", zfree[b]), ("dS", zdma[b])])
                zdma[b] = dma("sync", z_plain[mi * 128:(mi + 1) * 128, :], zst[b][:, :],
                              wait=[("sA", zev[b])])
                if mi == 2:
                    gld = dma("gpsimd", gth[:, :], cc1o.ap()[0:128, :],
                              wait=[("sC", cc1done)])
                    op("vector", lambda e: e.tensor_scalar(
                        out=gth[:, :], in0=gth[:, :], scalar1=c_sel[:, 0:1],
                        scalar2=None, op0=ALU.mult), wait=[("dG", gld)])
                    edma = 0
                    for ti in range(34):
                        op("vector", lambda e, ti=ti: e.tensor_copy(
                            xpe[:, 0:3], gth[:, ti * 3:(ti + 1) * 3]),
                            wait=[("dS", edma)])
                        op("vector", lambda e, ti=ti: e.tensor_copy(
                            xpe[:, 3:6], edge[:, ti * 3:(ti + 1) * 3]))
                        for o_col, base in ((6, 0), (7, 1), (2, 2)):
                            op("vector", lambda e, ti=ti, o=o_col, base=base:
                               e.tensor_scalar(
                                   out=xpe[:, o:o + 1], in0=xpe[:, base:base + 1],
                                   scalar1=c_cw[:, ti * 4:ti * 4 + 1], scalar2=None,
                                   op0=ALU.mult))
                            for kk in range(1, 4):
                                op("vector", lambda e, ti=ti, o=o_col, base=base, kk=kk:
                                   e.scalar_tensor_tensor(
                                       out=xpe[:, o:o + 1],
                                       in0=xpe[:, base + kk:base + kk + 1],
                                       scalar=c_cw[:, ti * 4 + kk:ti * 4 + kk + 1],
                                       in1=xpe[:, o:o + 1], op0=ALU.mult, op1=ALU.add))
                        v = op("vector", lambda e: e.tensor_copy(xpe[:, 0:2], xpe[:, 6:8]))
                        a = op("scalar", lambda e: e.activation(
                            xpe[:, 3:6], xpe[:, 0:3], AF.Silu), wait=[("sV", v)])
                        if ti < 32:
                            edma = dma("sync", xs_plain[ti * 128:(ti + 1) * 128, 0:3],
                                       xpe[:, 3:6], wait=[("sA", a)])
                        elif ti == 32:
                            op("vector", lambda e: e.tensor_copy(B_fm[:, 0:3], xpe[:, 3:6]),
                               wait=[("sA", a)])
                        else:
                            op("vector", lambda e: e.tensor_copy(C_fm[:, 0:3], xpe[:, 3:6]),
                               wait=[("sA", a)])
            barrier()

            if stop_after is not None and stop_after <= 6:
                raise _StopPlan()
            v = op("vector", lambda e: e.tensor_scalar(out=dtv, in0=dt_fm,
                    scalar1=c_dtb[:, 0:1], scalar2=None, op0=ALU.add))
            v = op("vector", lambda e: e.tensor_scalar(out=t2d, in0=dtv,
                    scalar1=-1.0, scalar2=None, op0=ALU.mult))
            v = op("vector", lambda e: e.tensor_max(t1, dtv, t2d))
            a = op("scalar", lambda e: e.activation(t1, t1, AF.Exp, scale=-1.0),
                   wait=[("sV", v)])
            aT = op("scalar", lambda e: e.activation(t1, t1, AF.Ln, bias=1.0),
                    selfwait=True)
            v = op("vector", lambda e: e.tensor_scalar(out=dt_soft, in0=dtv, scalar1=0.0,
                    scalar2=None, op0=ALU.max))
            vdt = op("vector", lambda e: e.tensor_add(dt_soft, dt_soft, t1),
                     wait=[("sA", aT)])
            aA = op("scalar", lambda e: e.activation(nA, c_al[:, 0:1], AF.Exp),
                    wait=[("sV", vdt)])
            vA = op("vector", lambda e: e.tensor_scalar(out=nA, in0=nA,
                    scalar1=-1.0, scalar2=None, op0=ALU.mult), wait=[("sA", aA)])
            vlog = op("vector", lambda e: e.tensor_scalar(out=logdA, in0=dt_soft,
                    scalar1=nA[:, 0:1], scalar2=None, op0=ALU.mult), wait=[("sV", vA)])
            for c in range(NCH):
                vscan = op("vector", lambda e, c=c: e.tensor_tensor_scan(
                    s_all[:, c * 128:(c + 1) * 128], logdA[:, c * 128:(c + 1) * 128],
                    z64[:, :], 0.0, ALU.add, ALU.add))
            v = op("vector", lambda e: e.tensor_copy(stot, s_all[:, 127:1024:128]),
                   selfwait=True)
            v = op("vector", lambda e: e.tensor_copy(seg[:, 7:8], stot[:, 7:8]),
                   selfwait=True)
            for c in range(6, -1, -1):
                v = op("vector", lambda e, c=c: e.tensor_add(
                    seg[:, c:c + 1], stot[:, c:c + 1], seg[:, c + 1:c + 2]),
                    selfwait=True)
            vseg = v
            op("scalar", lambda e: e.activation(lam_a, stot, AF.Exp), wait=[("sV", vseg)])
            tprev = [0, 0]
            vtr = 0
            for c in range(NCH):
                t = op("tensor", lambda e, c=c: e.transpose(
                    psAv64[0], s_all[:, c * 128:(c + 1) * 128], c_id[0:64, 0:64]),
                    wait=[("sV", max(vscan, tprev[0]))], inc=True)
                tprev[0] = op("vector", lambda e, c=c: e.tensor_copy(
                    s_tok[:, c * 64:(c + 1) * 64], psAv64[0]), wait=[("sT", t)])
                t = op("tensor", lambda e, c=c: e.transpose(
                    psAv64[1], dt_soft[:, c * 128:(c + 1) * 128], c_id[0:64, 0:64]),
                    wait=[("sV", max(vdt, tprev[1]))], inc=True)
                tprev[1] = op("vector", lambda e, c=c: e.tensor_copy(
                    dt_tok[:, c * 64:(c + 1) * 64], psAv64[1]), wait=[("sT", t)])
                vtr = tprev[1]
            d_f = dma("sync", rows_d[0:1, 4096:4608], stot, wait=[("sV", vseg)],
                      allow_slow_non_contiguous=True)
            dma("sync", b_stot, rows_d[0:1, 4096:4608].partition_broadcast(128),
                wait=[("dS", d_f)])
            d_f2 = dma("sync", rows_d[0:1, 4608:5120], seg, wait=[("sV", vseg)],
                       allow_slow_non_contiguous=True)
            dma("sync", b_seg, rows_d[0:1, 4608:5120].partition_broadcast(128),
                wait=[("dS", d_f2)])
            for c in range(NCH):
                dmat("sync", B_tok[:, c * 128:(c + 1) * 128],
                     B_fm[:, c * 128:(c + 1) * 128])
            barrier()

            if stop_after is not None and stop_after <= 7:
                raise _StopPlan()
            v_bsend = 0
            a_we = 0
            v_wd = 0
            t_wdfree = 0
            for c in range(NCH):
                d_x = dmat("sync", xst[:, :], xs_plain[:, c * 128:(c + 1) * 128],
                           wait=[("sV", v_wd)])
                v_bsend = op("vector", lambda e, c=c: e.tensor_sub(
                    d1, b_seg[:, c:512:8], s_tok[:, c * 64:(c + 1) * 64]),
                    wait=[("sA", a_we)])
                a_we = op("scalar", lambda e: e.activation(we_t, d1, AF.Exp),
                          wait=[("sV", v_bsend)])
                v = op("vector", lambda e, c=c: e.tensor_mul(
                    wdt_b[:, :], we_t, dt_tok[:, c * 64:(c + 1) * 64]),
                    wait=[("sA", a_we)])
                v_wd = op("vector", lambda e: e.tensor_tensor(
                    WD[:, :].rearrange("p (r q) -> p r q", r=64),
                    xst[:, :].rearrange("p (r q) -> p r q", r=64),
                    wdt_b[:, :, None].broadcast_to([128, 64, 64]), ALU.mult),
                    wait=[("sV", v), ("dS", d_x), ("sT", t_wdfree)])
                for q in range(8):
                    if q < 2:
                        o = psA[:, q * 512:(q + 1) * 512]
                    elif q < 4:
                        o = psB[:, (q - 2) * 512:(q - 1) * 512]
                    else:
                        o = psY[:, (q - 4) * 512:(q - 3) * 512]
                    t_wdfree = mm(o, B_tok[:, c * 128:(c + 1) * 128],
                       WD[:, q * 512:(q + 1) * 512],
                       start=(c == 0), stop=(c == NCH - 1),
                       wait=([("sV", v_wd)] if q == 0 else []), inc=(q == 7))
            v = op("vector", lambda e: e.tensor_copy(H_f[:, 0:1024], psA[:, :]),
                   wait=[("sT", t_wdfree)])
            v = op("vector", lambda e: e.tensor_copy(H_f[:, 1024:2048], psB[:, :]))
            vHev = op("vector", lambda e: e.tensor_copy(H_f[:, 2048:4096], psY[:, :]))
            d_h = dma("sync", cc2i.ap(), H_f[:, :], wait=[("sV", vHev)])
            cc2done = coll("AllGather", [cc2i.ap()], [cc2o.ap()], wait=[("dS", d_h)])

            if stop_after is not None and stop_after <= 8:
                raise _StopPlan()
            vprev = vHev
            for c in range(NCH):
                t = op("tensor", lambda e, c=c: e.matmul(
                    psA[:, 0:128], B_fm[:, c * 128:(c + 1) * 128],
                    C_fm[:, c * 128:(c + 1) * 128], start=True, stop=True,
                    skip_group_check=True),
                    wait=[("sV", vprev)], inc=True)
                vprev = op("vector", lambda e, c=c: e.tensor_mul(
                    CBm[:, c * 128:(c + 1) * 128], psA[:, 0:128], c_tri[:, :]),
                    wait=[("sT", t)])
            dma("gpsimd", ind_sb[:, :], c_ind_d)
            gld = dma("gpsimd", srow[:, :], cc2o.ap()[0:128, :], wait=[("sC", cc2done)])
            vHi = op("vector", lambda e: e.tensor_scalar(
                out=H_f[:, :], in0=srow[:, :], scalar1=c_sel[:, 0:1], scalar2=None,
                op0=ALU.mult), wait=[("dG", gld)])
            vHbf = op("vector", lambda e: e.tensor_copy(Hbf[:, :], H_f[:, :]))
            barrier()

            if stop_after is not None and stop_after <= 9:
                raise _StopPlan()
            d_f = dma("sync", srow_d[0:1, :], s_all[0:32, 0:128])
            dsr = dma("sync", srow[:, :], srow_d.partition_broadcast(128),
                      wait=[("dS", d_f)])
            dszt = dmat("sync", szt[:, :], z_plain[:, 0:128])
            dxt = dmat("sync", xst[:, :], xs_plain[:, 0:128])
            dygw = 0
            tlam = 0
            v_t2 = 0
            v_y1ev = 0
            a_we = 0
            vg = 0
            tY_last = 0
            t_S_last = 0
            for c in range(NCH):
                v_stot = op("vector", lambda e, c=c: e.tensor_sub(
                    d1, b_stot[:, c:512:8], s_tok[:, c * 64:(c + 1) * 64]),
                    wait=[("sA", a_we)])
                a_we = op("scalar", lambda e: e.activation(we_t, d1, AF.Exp),
                          wait=[("sV", v_stot)])
                a_es = op("scalar", lambda e, c=c: e.activation(
                    es_t, s_tok[:, c * 64:(c + 1) * 64], AF.Exp), wait=[("sV", v_t2)])
                for h in range(2):
                    vD = op("vector", lambda e, c=c, h=h: e.tensor_tensor(
                        srow[:, :].rearrange("p (r i) -> p r i", r=32),
                        srow[:, :].rearrange("p (r i) -> p r i", r=32),
                        s_tok[:, c * 64 + 32 * h:c * 64 + 32 * h + 32][:, :, None]
                        .broadcast_to([128, 32, 128]), ALU.subtract),
                        wait=[("dS", dsr)])
                    vD = op("vector", lambda e: e.tensor_scalar(
                        out=srow[:, :], in0=srow[:, :], scalar1=0.0, scalar2=None,
                        op0=ALU.min), selfwait=True)
                    aE = op("scalar", lambda e: e.activation(
                        E_sb[:, :], srow[:, :], AF.Exp),
                        wait=[("sV", vD), ("sT", tY_last), ("dS", dygw)])
                    if h == 0:
                        d_f = dma("sync", srow_d[0:1, :],
                                  s_all[32:64, c * 128:(c + 1) * 128], wait=[("sA", aE)])
                        dsr = dma("sync", srow[:, :], srow_d.partition_broadcast(128),
                                  wait=[("dS", d_f)])
                    elif c < NCH - 1:
                        d_f = dma("sync", srow_d[0:1, :],
                                  s_all[0:32, (c + 1) * 128:(c + 2) * 128],
                                  wait=[("sA", aE)])
                        dsr = dma("sync", srow[:, :], srow_d.partition_broadcast(128),
                                  wait=[("dS", d_f)])
                    vM = op("vector", lambda e, c=c: e.tensor_tensor(
                        E_sb[:, :].rearrange("p (r i) -> p r i", r=32),
                        E_sb[:, :].rearrange("p (r i) -> p r i", r=32),
                        CBm[:, c * 128:(c + 1) * 128][:, None, :]
                        .broadcast_to([128, 32, 128]), ALU.mult), wait=[("sA", aE)])
                    vX = op("vector", lambda e, c=c, h=h: e.tensor_tensor(
                        XD[:, h * 2048:(h + 1) * 2048].rearrange("p (r q) -> p r q", r=32),
                        xst[:, h * 2048:(h + 1) * 2048].rearrange("p (r q) -> p r q", r=32),
                        dt_tok[:, c * 64 + 32 * h:c * 64 + 32 * h + 32][:, :, None]
                        .broadcast_to([128, 32, 64]), ALU.mult),
                        wait=([("dS", dxt)] if h == 0 else []) +
                        [("sV", v_y1ev), ("sT", t_S_last)])
                    for q in range(4):
                        o = psA[:, q * 512:(q + 1) * 512] if q < 2 else                         psB[:, (q - 2) * 512:(q - 1) * 512]
                        mm(o, C_fm[:, c * 128:(c + 1) * 128],
                           Hbf[:, h * 2048 + q * 512:h * 2048 + (q + 1) * 512],
                           start=True, stop=True,
                           wait=[("sV", v_t2), ("sV", vHbf)] if q == 0 else [])
                    for jl in range(32):
                        r = h * 32 + jl
                        tY = mm(psY[:, jl * 64:(jl + 1) * 64],
                           E_sb[:, jl * 128:(jl + 1) * 128],
                           XD[:, r * 64:(r + 1) * 64],
                           start=True, stop=True,
                           wait=([("sV", vM), ("sV", vX), ("sV", v_y1ev)]
                           if jl == 0 else []), inc=(jl == 31))
                    tY_last = tY
                    v_dxs = op("vector", lambda e, h=h: e.tensor_tensor(
                        XW[:, h * 2048:(h + 1) * 2048], xst[:, h * 2048:(h + 1) * 2048],
                        Db[:, h * 2048:(h + 1) * 2048], ALU.mult), wait=[("sT", tY)])
                    v_y1ev = op("vector", lambda e, h=h: e.tensor_add(
                        y_half[:, :], XW[:, h * 2048:(h + 1) * 2048], psY[:, 0:2048]),
                        wait=[("sV", v_dxs)])
                    v = op("vector", lambda e, h=h: e.tensor_tensor(
                        Htmp[:, :], psA[:, :],
                        es_t[:, h * 32:h * 32 + 16][:, :, None]
                        .broadcast_to([128, 16, 64]), ALU.mult),
                        wait=[("sA", a_es), ("sV", v_y1ev)])
                    v = op("vector", lambda e: e.tensor_add(
                        y_half[:, 0:1024], y_half[:, 0:1024], Htmp[:, :]))
                    v = op("vector", lambda e, h=h: e.tensor_tensor(
                        Htmp[:, :], psB[:, :],
                        es_t[:, h * 32 + 16:h * 32 + 32][:, :, None]
                        .broadcast_to([128, 16, 64]), ALU.mult))
                    v_t2 = op("vector", lambda e: e.tensor_add(
                        y_half[:, 1024:2048], y_half[:, 1024:2048], Htmp[:, :]))
                    vg = op("vector", lambda e, h=h: e.tensor_tensor(
                        yg[:, h * 2048:(h + 1) * 2048], y_half[:, :],
                        szt[:, h * 2048:(h + 1) * 2048], ALU.mult),
                        wait=[("dS", dszt)])
                aS = op("scalar", lambda e: e.activation(
                    E_sb[:, :], yg[:, :], AF.Square),
                    wait=[("sV", vg), ("sT", tY_last)])
                v_ms = op("vector", lambda e: e.reduce_sum(
                    ms_col, E_sb[:, :], mybir.AxisListType.X), wait=[("sA", aS)])
                a = op("scalar", lambda e: e.activation(rs_col, ms_col, AF.Ln,
                        bias=EPS, scale=1.0 / 4096), wait=[("sV", v_ms)])
                a_rs = op("scalar", lambda e: e.activation(rs_col, rs_col, AF.Exp,
                        scale=-0.5), selfwait=True)
                v_w = op("vector", lambda e: e.tensor_scalar(
                    out=E_sb[:, :], in0=yg[:, :], scalar1=rs_col, scalar2=None,
                    op0=ALU.mult), wait=[("sA", a_rs)])
                dygw = dma("sync", ygw_dr[c * 128:(c + 1) * 128, :], E_sb[:, :],
                           wait=[("sV", v_w)])
                v_wdt = op("vector", lambda e, c=c: e.tensor_mul(
                    wdt_b[:, :], we_t, dt_tok[:, c * 64:(c + 1) * 64]),
                    wait=[("sA", a_we)])
                vWD = op("vector", lambda e: e.tensor_tensor(
                    WD[:, :].rearrange("p (r q) -> p r q", r=64),
                    xst[:, :].rearrange("p (r q) -> p r q", r=64),
                    wdt_b[:, :, None].broadcast_to([128, 64, 64]), ALU.mult),
                    wait=[("sV", v_wdt), ("sV", v_y1ev)])
                vL = op("vector", lambda e, c=c: e.tensor_scalar(
                    out=lamT[:, :], in0=c_ones64[:, :], scalar1=lam_a[:, c:c + 1],
                    scalar2=None, op0=ALU.mult), wait=[("sT", tlam)])
                vHq = 0
                for q in range(4):
                    for hh in range(2):
                        tlam = mm(psB[:, hh * 512:(hh + 1) * 512], lamT[:, :],
                                  ind_sb[:, q * 1024 + hh * 512:q * 1024 + (hh + 1) * 512],
                                  start=True, stop=(hh == 1),
                                  wait=([("sV", vL), ("sV", vHq)] +
                                        ([("sV", v_t2)] if q == 0 else []))
                                  if hh == 0 else [],
                                  inc=(hh == 1))
                    vt = op("vector", lambda e, q=q: e.tensor_tensor(
                        Htmp[:, :], psB[:, :], H_f[:, q * 1024:(q + 1) * 1024], ALU.mult),
                        wait=[("sT", tlam)])
                    for hh in range(2):
                        tS = mm(psA[:, hh * 512:(hh + 1) * 512],
                                B_tok[:, c * 128:(c + 1) * 128],
                                WD[:, q * 1024 + hh * 512:q * 1024 + (hh + 1) * 512],
                                start=True, stop=(hh == 1),
                                wait=([("sV", vWD), ("sV", vHq)] +
                                      ([("sV", v_t2)] if q == 0 else []))
                                if hh == 0 else [],
                                inc=(hh == 1))
                    vHq = op("vector", lambda e, q=q: e.tensor_add(
                        H_f[:, q * 1024:(q + 1) * 1024], Htmp[:, :], psA[:, :]),
                        wait=[("sT", tS), ("sV", vt)])
                    t_S_last = tS
                vHbf = op("vector", lambda e: e.tensor_copy(Hbf[:, :], H_f[:, :]),
                          wait=[("sV", vHq)])
                if c < NCH - 1:
                    dxt = dmat("sync", xst[:, :],
                               xs_plain[:, (c + 1) * 128:(c + 2) * 128],
                               wait=[("sV", vWD)])
                    dszt = dmat("sync", szt[:, :],
                                z_plain[:, (c + 1) * 128:(c + 2) * 128],
                                wait=[("sV", vg)])
            barrier()
            if debug:
                dma("sync", d_s, s_all)
                dma("sync", d_dt, dt_fm)
                dma("sync", d_sm[:, 0:8], stot)
                dma("sync", d_sm[:, 8:16], seg)
                dma("sync", d_sm[:, 16:24], lam_a)
                dma("sync", d_sm[:, 24:25], nA, allow_slow_non_contiguous=True)
                dma("sync", d_ds, dt_soft)
                dma("sync", d_ld, logdA)
                dma("sync", d_H, cc2i.ap())
                dma("sync", d_Hg, cc2o.ap())
                dma("sync", d_ygw, ygw_dr)
                dma("sync", d_xs, xs_plain)
                dma("sync", d_z, z_plain)
                dma("sync", d_ffn, ffn_dr)
                barrier()

            if stop_after is not None and stop_after <= 10:
                raise _StopPlan()
            dyl = 0
            for kc in range(KC):
                dyl = dmat("sync", AB1[:, kc * 1024:(kc + 1) * 1024],
                           ygw_dr[:, kc * 128:(kc + 1) * 128])
            wfree = [0, 0]
            vsc = [0, 0]
            vev = [0, 0]
            odma = [0, 0]
            for mi in range(KC):
                b = mi % 2
                wld[b] = dma("gpsimd", wf32[b], w_out[mi], wait=[("sV", vsc[b])])
                dx = dma("sync", x2[b], xin[mi * 128:(mi + 1) * 128, :],
                         wait=[("sV", vev[b])])
                df = dma("sync", ffn2[b], ffn_dr[mi * 128:(mi + 1) * 128, :])
                vsc[b] = op("vector", lambda e, b=b: e.tensor_tensor(
                    wb[b][:, :].rearrange("p (kc m) -> p kc m", kc=KC),
                    wf32[b].rearrange("p (kc m) -> p kc m", kc=KC),
                    c_nw[:, :, None].broadcast_to([128, KC, 128]), ALU.mult),
                    wait=[("dG", wld[b]), ("sT", wfree[b])])
                ps = psA if b == 0 else psB
                for kc in range(KC):
                    for th in range(2):
                        wfree[b] = mm(ps[:, th * 512:(th + 1) * 512],
                           wb[b][:, kc * 128:(kc + 1) * 128],
                           AB1[:, kc * 1024 + th * 512:kc * 1024 + (th + 1) * 512],
                           start=(kc == 0), stop=(kc == KC - 1),
                           wait=([("sV", vsc[b]), ("dS", dyl), ("sV", vev[b])]
                           if kc == 0 and th == 0 else []),
                           inc=(kc == KC - 1 and th == 1))
                v = op("vector", lambda e, ps=ps, b=b: e.tensor_add(o2[b], ps[:, :], x2[b]),
                       wait=[("sT", wfree[b]), ("dS", df), ("dS", odma[b])])
                vev[b] = op("vector", lambda e, b=b: e.tensor_add(o2[b], o2[b], ffn2[b]))
                odma[b] = dma("sync", out_fm[mi * 128:(mi + 1) * 128, :], o2[b],
                              wait=[("sV", vev[b])])

        except _StopPlan:
            pass
        _w("sync", [("dS", C["dS"])])
        _w("gpsimd", [("dG", C["dG"])])

        block = En(nc.Block())

        @block.sync
        def _(e):
            for f in prog["sync"]:
                f(e)

        @block.scalar
        def _(e):
            for f in prog["scalar"]:
                f(e)

        @block.vector
        def _(e):
            for f in prog["vector"]:
                f(e)

        @block.tensor
        def _(e):
            for f in prog["tensor"]:
                f(e)

        @block.gpsimd
        def _(e):
            for f in prog["gpsimd"]:
                f(e)

    return nc


def _host_prep(inputs):
    f32 = np.float32
    W = np.asarray(inputs["L0_attn_in_proj_w"], f32)
    Wp = np.concatenate([W, np.zeros((4096, 64), f32)], axis=1)
    w_in = np.ascontiguousarray(Wp.reshape(32, 128, MC, 128).transpose(2, 1, 0, 3))
    Wo = np.asarray(inputs["L0_attn_out_proj_w"], f32)
    w_out = np.ascontiguousarray(Wo.reshape(32, 128, 32, 128).transpose(2, 1, 0, 3))
    q = np.asarray(inputs["L0_ffn_qweight"], np.int32)
    k = np.arange(4096)
    src = 3 * (k // 4) + np.maximum(k % 4 - 1, 0)
    qe = np.ascontiguousarray(
        q[:, src].T.reshape(32, 128, 32, 128).transpose(2, 1, 0, 3))
    cw = np.asarray(inputs["L0_attn_conv_w"], f32)
    convd = np.zeros((34, 128, 4, 128), f32)
    ii = np.arange(128)
    for ti in range(34):
        for kk in range(4):
            convd[ti, ii, kk, ii] = cw[ti * 128:(ti + 1) * 128, kk]
    convd = convd.astype(NBF)
    mup = np.asarray(inputs["L0_attn_mup_vector"], f32)
    mupp = np.concatenate([mup, np.zeros(64, f32)])
    c_mup = np.ascontiguousarray(mupp.reshape(MC, 128).T)
    sc = np.asarray(inputs["L0_ffn_weight_scale"], f32).reshape(4096)
    zp = np.asarray(inputs["L0_ffn_weight_zero_point"], f32).reshape(4096)
    fb = np.asarray(inputs["L0_ffn_bias"], f32)
    c_sc = np.ascontiguousarray(sc.reshape(32, 128).T)
    c_nszp = np.ascontiguousarray((-sc * zp).reshape(32, 128).T)
    c_fb = np.ascontiguousarray(fb.reshape(32, 128).T)
    D = np.asarray(inputs["L0_attn_D"], f32)
    c_df = np.repeat(D, 64).reshape(1, 4096).astype(NBF)
    nw = np.asarray(inputs["L0_attn_norm_w"], f32)
    c_nw = np.ascontiguousarray(nw.reshape(32, 128).T)
    lnw = np.asarray(inputs["L0_attn_ln_w"], f32)
    c_lnw = np.ascontiguousarray(lnw.reshape(32, 128).T)
    c_cw = np.zeros((128, 136), f32)
    for ti in range(34):
        c_cw[:, ti * 4:(ti + 1) * 4] = cw[ti * 128:(ti + 1) * 128, :]
    c_sh = ((np.arange(128) % 4 == 1) * 6).astype(np.int32).reshape(128, 1)
    c_dtb = np.asarray(inputs["L0_attn_dt_bias"], f32).reshape(64, 1)
    c_al = np.asarray(inputs["L0_attn_A_log"], f32).reshape(64, 1)
    c_id = np.eye(128, dtype=f32)
    jj = np.arange(128)
    c_tri = (jj[:, None] <= jj[None, :]).astype(f32)
    c_ind = np.repeat(np.eye(64, dtype=f32), 64, axis=1).astype(NBF)

    shared = dict(w_in=w_in, w_out=w_out, qexp=qe, convd=convd, c_id=c_id,
                  c_tri=c_tri, c_ind=c_ind, c_mup=c_mup, c_sc=c_sc,
                  c_nszp=c_nszp, c_fb=c_fb, c_df=c_df, c_nw=c_nw, c_lnw=c_lnw,
                  c_cw=c_cw, c_sh=c_sh, c_dtb=c_dtb, c_al=c_al,
                  c_cv=np.array([[EPS, -0.5, -1.0, 1.0, 1.0 / 4096, 0.0, 0.0, 0.0]],
                                np.float32))
    _ = inputs["L0_attn_residual"]
    x = np.asarray(inputs["L0_attn_hidden_states"], np.float32)
    in_maps = []
    for core in range(8):
        b, h = core // 2, core % 2
        xc = np.ascontiguousarray(x[b, h * 1024:(h + 1) * 1024, :].T)
        sel = np.full((128, 1), 1.0 if h == 1 else 0.0, np.float32)
        m = dict(shared)
        m["xin"] = xc
        m["c_sel"] = sel
        in_maps.append(m)
    return in_maps


def kernel(**inputs):
    if "nc" not in _NC_CACHE:
        _NC_CACHE["nc"] = build_nc()
    nc = _NC_CACHE["nc"]
    in_maps = _host_prep(inputs)
    res = run_bass_kernel_spmd(nc, in_maps, core_ids=list(range(8)))
    out = np.empty((4, 2048, 4096), np.float32)
    for core in range(8):
        b, h = core // 2, core % 2
        out[b, h * 1024:(h + 1) * 1024, :] = res.results[core]["out_fm"].T
    return out


# version 08
# speedup vs baseline: 1.1030x; 1.1030x over previous
import contextlib

import numpy as np
import ml_dtypes

import concourse.bass as bass
import concourse.mybir as mybir
from concourse.bass_utils import run_bass_kernel_spmd

EPS = 1e-5
F32 = mybir.dt.float32
BF16 = mybir.dt.bfloat16
I32 = mybir.dt.int32
AF = mybir.ActivationFunctionType
ALU = mybir.AluOpType
NBF = ml_dtypes.bfloat16

T = 1024
KC = 32
MC = 67
NCH = 8
ENGINES = ("sync", "scalar", "vector", "tensor", "gpsimd")
RG = [[0, 1], [2, 3], [4, 5], [6, 7]]

_NC_CACHE = {}


class _StopPlan(Exception):
    pass


def build_nc(stop_after=None, debug=False):
    nc = bass.Bass("TRN2", target_bir_lowering=False)

    xin = nc.dram_tensor("xin", [4096, T], F32, kind="ExternalInput").ap()
    w_in = nc.dram_tensor("w_in", [MC, 128, KC, 128], F32, kind="ExternalInput").ap()
    w_out = nc.dram_tensor("w_out", [KC, 128, KC, 128], F32, kind="ExternalInput").ap()
    qexp = nc.dram_tensor("qexp", [KC, 128, KC, 128], I32, kind="ExternalInput").ap()
    convd = nc.dram_tensor("convd", [34, 128, 4, 128], BF16, kind="ExternalInput").ap()
    c_id_d = nc.dram_tensor("c_id", [128, 128], F32, kind="ExternalInput").ap()
    c_tri_d = nc.dram_tensor("c_tri", [128, 128], F32, kind="ExternalInput").ap()
    c_ind_d = nc.dram_tensor("c_ind", [64, 4096], BF16, kind="ExternalInput").ap()
    c_mup_d = nc.dram_tensor("c_mup", [128, MC], F32, kind="ExternalInput").ap()
    c_sc_d = nc.dram_tensor("c_sc", [128, KC], F32, kind="ExternalInput").ap()
    c_nszp_d = nc.dram_tensor("c_nszp", [128, KC], F32, kind="ExternalInput").ap()
    c_fb_d = nc.dram_tensor("c_fb", [128, KC], F32, kind="ExternalInput").ap()
    c_df_d = nc.dram_tensor("c_df", [1, 4096], BF16, kind="ExternalInput").ap()
    c_nw_d = nc.dram_tensor("c_nw", [128, KC], F32, kind="ExternalInput").ap()
    c_lnw_d = nc.dram_tensor("c_lnw", [128, KC], F32, kind="ExternalInput").ap()
    c_cw_d = nc.dram_tensor("c_cw", [128, 136], F32, kind="ExternalInput").ap()
    c_sh_d = nc.dram_tensor("c_sh", [128, 1], I32, kind="ExternalInput").ap()
    c_dtb_d = nc.dram_tensor("c_dtb", [64, 1], F32, kind="ExternalInput").ap()
    c_al_d = nc.dram_tensor("c_al", [64, 1], F32, kind="ExternalInput").ap()
    c_sel_d = nc.dram_tensor("c_sel", [128, 1], F32, kind="ExternalInput").ap()
    c_cv_d = nc.dram_tensor("c_cv", [1, 8], F32, kind="ExternalInput").ap()
    out_fm = nc.dram_tensor("out_fm", [4096, T], F32, kind="ExternalOutput").ap()
    if debug:
        d_xs = nc.dram_tensor("d_xs", [4096, T], BF16, kind="ExternalOutput").ap()
        d_z = nc.dram_tensor("d_z", [4096, T], BF16, kind="ExternalOutput").ap()
        d_ffn = nc.dram_tensor("d_ffn", [4096, T], F32, kind="ExternalOutput").ap()
        d_ygw = nc.dram_tensor("d_ygw", [T, 4096], BF16, kind="ExternalOutput").ap()
        d_H = nc.dram_tensor("d_H", [128, 4096], F32, kind="ExternalOutput").ap()
        d_Hg = nc.dram_tensor("d_Hg", [256, 4096], F32, kind="ExternalOutput").ap()
        d_dt = nc.dram_tensor("d_dt", [64, T], F32, kind="ExternalOutput").ap()
        d_s = nc.dram_tensor("d_s", [64, T], F32, kind="ExternalOutput").ap()
        d_sm = nc.dram_tensor("d_sm", [64, 64], F32, kind="ExternalOutput").ap()
        d_ds = nc.dram_tensor("d_ds", [64, T], F32, kind="ExternalOutput").ap()
        d_ld = nc.dram_tensor("d_ld", [64, T], F32, kind="ExternalOutput").ap()

    z_plain = nc.dram_tensor("z_plain", [4096, T], BF16).ap()
    xs_plain = nc.dram_tensor("xs_plain", [4096, T], BF16).ap()
    ffn_dr = nc.dram_tensor("ffn_dr", [4096, T], F32).ap()
    ygw_dr = nc.dram_tensor("ygw_dr", [T, 4096], BF16).ap()
    rows_d = nc.dram_tensor("rows_d", [1, 8192], F32).ap()
    srow_d = nc.dram_tensor("srow_d", [1, 4096], F32).ap()
    cc1i = nc.dram_tensor("cc1i", [128, 102], BF16)
    cc1o = nc.dram_tensor("cc1o", [256, 102], BF16)
    cc2i = nc.dram_tensor("cc2i", [128, 4096], F32)
    cc2o = nc.dram_tensor("cc2o", [256, 4096], F32)

    with contextlib.ExitStack() as es:
        En = es.enter_context

        AB1 = En(nc.sbuf_tensor("AB1", [128, 32768], BF16))
        AB2 = En(nc.sbuf_tensor("AB2", [128, 24576], BF16))
        AB3 = En(nc.sbuf_tensor("AB3", [128, 8192], BF16))
        AB4 = En(nc.sbuf_tensor("AB4", [128, 12288], BF16))
        AFR = En(nc.sbuf_tensor("AFR", [128, 7168], F32))
        c_id = En(nc.sbuf_tensor("s_id", [128, 128], F32))
        c_tri = En(nc.sbuf_tensor("s_tri", [128, 128], F32))
        c_mup = En(nc.sbuf_tensor("s_mup", [128, MC], F32))
        c_sc = En(nc.sbuf_tensor("s_sc", [128, KC], F32))
        c_nszp = En(nc.sbuf_tensor("s_nszp", [128, KC], F32))
        c_fb = En(nc.sbuf_tensor("s_fb", [128, KC], F32))
        Db = En(nc.sbuf_tensor("s_Db", [128, 4096], BF16))
        c_nw = En(nc.sbuf_tensor("s_nw", [128, KC], F32))
        c_lnw = En(nc.sbuf_tensor("s_lnw", [128, KC], F32))
        c_cw = En(nc.sbuf_tensor("s_cw", [128, 136], F32))
        c_sh = En(nc.sbuf_tensor("s_sh", [128, 1], I32))
        c_dtb = En(nc.sbuf_tensor("s_dtb", [64, 1], F32))
        c_al = En(nc.sbuf_tensor("s_al", [64, 1], F32))
        c_sel = En(nc.sbuf_tensor("s_sel", [128, 1], F32))
        c_ones = En(nc.sbuf_tensor("s_ones", [128, 1], BF16))
        c_o128 = En(nc.sbuf_tensor("s_o128", [128, 128], BF16))
        c_ones64 = En(nc.sbuf_tensor("s_ones64", [64, 128], F32))
        z64 = En(nc.sbuf_tensor("s_z64", [64, 128], F32))
        c_cv = En(nc.sbuf_tensor("s_cv", [128, 8], F32))

        psA = En(nc.psum_tensor("psA", [128, 1024], F32))
        psB = En(nc.psum_tensor("psB", [128, 1024], F32))
        psY = En(nc.psum_tensor("psY", [128, 2048], F32))
        psAv64 = [psA[:, 0:64], psB[:, 0:64]]

        sem = {}
        for s in ("dS", "dG", "sT", "sV", "sA", "sP", "sC"):
            sem[s] = En(nc.semaphore(s))

        AB1f = AB1[:, :].bitcast(F32)
        xn = AB1
        srow = AB1f[:, 0:4096]
        H_f = AB1f[:, 4096:8192]
        y_half = AB1f[:, 8192:10240]
        Htmp = AB1f[:, 10240:11264]
        ind_sb = AB3[0:64, 0:4096]

        AB2i = AB2[:, :].bitcast(I32)
        qx = [AB2i[:, 0:4096], AB2i[:, 4096:8192]]
        E_sb = AB2[:, 0:4096]
        XW = AB2[:, 4096:8192]
        XD = XW
        WD = XW
        Hbf = AB2[:, 8192:12288]
        xst = AB2[:, 12288:16384]
        yg = AB2[:, 16384:20480]
        szt = AB2[:, 20480:24576]

        wb = [AB3[:, 0:4096], AB3[:, 4096:8192]]
        vb = wb
        wb4 = [AB3[:, 0:4096], AB3[:, 4096:8192], AB2[:, 0:4096], AB2[:, 4096:8192]]

        B_fm = AB4[:, 0:1024]
        C_fm = AB4[:, 1024:2048]
        B_tok = AB4[:, 2048:3072]
        CBm = AB4[:, 3072:4096]
        dt_tok = AB4[:, 4096:4608]
        xb2 = [AB4[:, 4608:5632], AB4[:, 5632:6656]]
        zst = xb2
        sq2 = [AB4[:, 6656:7680], AB4[:, 7680:8704]]
        xsst = sq2
        xbc2 = [AB4[:, 8704:9728], AB4[:, 9728:10752]]
        dg2 = [AB4[:, 10752:11264], AB4[:, 11264:11776]]
        edge = AB4[:, 11776:11878]
        sendb = AB4[:, 11878:11980]
        xpe = AB4[:, 11980:11988]
        lamT = AB4[0:64, 11988:12116]
        gth = AB4[:, 12116:12218]
        wdt_b = AB4[:, 12218:12282]

        x2 = [AFR[:, 0:1024], AFR[:, 1024:2048]]
        b_rstd = AFR[:, 2048:3072]
        b_mr = AFR[:, 3072:4096]
        ffn2 = [AFR[:, 2048:3072], AFR[:, 3072:4096]]
        r_mean = AFR[0:1, 4096:5120]
        r_tmp = AFR[0:1, 5120:6144]
        r_rstd = AFR[0:1, 6144:7168]
        r_mr = r_mean
        r_rs1 = AFR[0:1, 4096:5120]
        r_cols = AFR[0:1, 6144:7168]
        b_stot = AFR[:, 4096:4608]
        b_send = AFR[:, 4608:4672]
        d1 = AFR[:, 4672:4736]
        we_t = AFR[:, 4736:4800]
        es_t = AFR[:, 4800:4864]
        rs_col = AFR[:, 4864:4865]
        ms_col = AFR[:, 4872:4873]
        o2 = [AFR[:, 4096:5120], AFR[:, 5120:6144]]
        b_rs1 = AFR[:, 6144:7168]
        dt_fm = AFR[0:64, 6144:7168]
        s_all = AB1f[0:64, 11264:12288]
        dtv = AB1f[0:64, 12288:13312]
        dt_soft = dtv
        t1 = AB1f[0:64, 13312:14336]
        logdA = t1
        s_tok = AB1f[:, 14336:14848]
        stot = AB1f[0:64, 14848:14856]
        seg = AB1f[0:64, 14864:14872]
        lam_a = AB1f[0:64, 14880:14888]
        nA = AB1f[0:64, 14896:14897]
        t2d = AB1f[0:64, 15360:16384]
        b_cols = AFR[:, 5120:6144]

        prog = {k: [] for k in ENGINES}
        PLAN = {k: [] for k in ENGINES}
        nc._plan = PLAN
        C = {s: 0 for s in sem}
        esem = {"tensor": "sT", "vector": "sV", "scalar": "sA", "gpsimd": "sP"}

        def _w(eng, waits):
            for n, v in waits:
                if v > 0:
                    PLAN[eng].append(("w", n, v))
                    prog[eng].append(lambda e, s=sem[n], vv=v: e.wait_ge(s, vv))

        def dma(q, out, in_, wait=(), **kw):
            _w(q, wait)
            sn = "dS" if q == "sync" else "dG"
            C[sn] += 16
            PLAN[q].append(("i", sn, 16))
            prog[q].append(
                lambda e, o=out, i=in_, k2=dict(kw), s=sem[sn]:
                e.dma_start(out=o, in_=i, **k2).then_inc(s, 16))
            return C[sn]

        def dmat(q, out, in_, wait=()):
            _w(q, wait)
            sn = "dS" if q == "sync" else "dG"
            C[sn] += 16
            PLAN[q].append(("i", sn, 16))
            prog[q].append(
                lambda e, o=out, i=in_, s=sem[sn]:
                e.dma_start_transpose(o, i).then_inc(s, 16))
            return C[sn]

        def op(eng, builder, wait=(), inc=True, selfwait=False):
            if selfwait:
                wait = list(wait) + [(esem[eng], C[esem[eng]])]
            _w(eng, wait)
            sn = esem[eng]
            if inc:
                C[sn] += 1
                PLAN[eng].append(("i", sn, 1))
                prog[eng].append(lambda e, b=builder, s=sem[sn]: b(e).then_inc(s, 1))
                return C[sn]
            PLAN[eng].append(("i", None, 0))
            prog[eng].append(lambda e, b=builder: b(e))
            return C[sn]

        def tmark(wait=()):
            return op("tensor", lambda e: e.nop(), wait=wait)

        def mm(out, lhsT, rhs, start, stop, wait=(), inc=False):
            return op("tensor",
                      lambda e, o=out, l=lhsT, r=rhs, st=start, sp=stop:
                      e.matmul(o, l, r, start=st, stop=sp, skip_group_check=True),
                      wait=wait, inc=inc)

        def coll(kind, ins, outs, wait=()):
            _w("gpsimd", wait)
            C["sC"] += 1
            PLAN["gpsimd"].append(("i", "sC", 1))
            prog["gpsimd"].append(
                lambda e, k=kind, i=ins, o=outs, s=sem["sC"]:
                e.collective_compute(k, ALU.bypass, replica_groups=RG, ins=i, outs=o)
                .then_inc(s, 1))
            return C["sC"]

        def barrier():
            snap = [(n, C[n]) for n in C]
            for eng in ENGINES:
                _w(eng, snap)

        try:
            for dst, src in [(c_id, c_id_d), (c_tri, c_tri_d), (c_mup, c_mup_d),
                             (c_sc, c_sc_d), (c_nszp, c_nszp_d), (c_fb, c_fb_d),
                             (c_nw, c_nw_d), (c_lnw, c_lnw_d),
                             (c_cw, c_cw_d), (c_sh, c_sh_d), (c_dtb, c_dtb_d),
                             (c_al, c_al_d), (c_sel, c_sel_d)]:
                dma("gpsimd", dst[:, :], src)
            dma("sync", c_cv[:, :], c_cv_d.partition_broadcast(128))
            dma("sync", Db[:, :], c_df_d.partition_broadcast(128))
            for _i, _v in enumerate([EPS, -0.5, -1.0, 1.0, 1.0 / 4096]):
                nc.const_aps.register(F32, _v, c_cv[:, _i:_i + 1])
            op("vector", lambda e: e.memset(c_ones[:, :], 1.0))
            op("vector", lambda e: e.memset(c_o128[:, :], 1.0))
            op("vector", lambda e: e.memset(c_ones64[:, :], 1.0))
            op("vector", lambda e: e.memset(z64[:, :], 0.0))
            barrier()

            if stop_after is not None and stop_after <= 0:
                raise _StopPlan()
            tdone = [0, 0]
            for k in range(KC):
                b = k % 2
                ld = dma("sync", x2[b], xin[k * 128:(k + 1) * 128, :],
                         wait=[("sT", tdone[b])])
                cv = op("vector", lambda e, b=b: e.tensor_copy(xb2[b][:, :], x2[b]),
                        wait=[("dS", ld), ("sT", tdone[b])])
                sv = op("scalar", lambda e, b=b: e.activation(sq2[b][:, :], x2[b], AF.Square),
                        wait=[("dS", ld), ("sT", tdone[b])])
                for h in range(2):
                    mm(psA[:, h * 512:(h + 1) * 512], c_o128[:, :],
                       xb2[b][:, h * 512:(h + 1) * 512], start=(k == 0), stop=(k == KC - 1),
                       wait=([("sV", cv), ("sA", sv)] if h == 0 else []))
                    tdone[b] = mm(psB[:, h * 512:(h + 1) * 512], c_o128[:, :],
                       sq2[b][:, h * 512:(h + 1) * 512], start=(k == 0), stop=(k == KC - 1),
                       inc=(h == 1))
            tstat = tdone[(KC - 1) % 2]
            v = op("vector", lambda e: e.tensor_copy(b_mr, psA[:, :]),
                   wait=[("sT", tstat)])
            v = op("vector", lambda e: e.tensor_copy(b_rstd, psB[:, :]))
            v = op("vector", lambda e: e.tensor_scalar(out=b_mr, in0=b_mr,
                    scalar1=1.0 / 4096, scalar2=None, op0=ALU.mult))
            v = op("vector", lambda e: e.tensor_scalar(out=b_rstd, in0=b_rstd,
                    scalar1=1.0 / 4096, scalar2=None, op0=ALU.mult))
            v = op("vector", lambda e: e.tensor_mul(x2[0][:, :], b_mr, b_mr))
            v = op("vector", lambda e: e.tensor_sub(b_rstd, b_rstd, x2[0][:, :]))
            a = op("scalar", lambda e: e.activation(b_rstd, b_rstd, AF.Ln, bias=EPS),
                   wait=[("sV", v)])
            a = op("scalar", lambda e: e.activation(b_rstd, b_rstd, AF.Exp, scale=-0.5),
                   selfwait=True)
            v = op("vector", lambda e: e.tensor_mul(b_mr, b_mr, b_rstd), wait=[("sA", a)])
            barrier()

            if stop_after is not None and stop_after <= 1:
                raise _StopPlan()
            tdone = [0, 0]
            for k in range(KC):
                b = k % 2
                ld = dma("sync", x2[b], xin[k * 128:(k + 1) * 128, :],
                         wait=[("sT", tdone[b])])
                v = op("vector", lambda e, b=b: e.tensor_mul(x2[b], x2[b], b_rstd),
                       wait=[("dS", ld)])
                v = op("vector", lambda e, b=b, k=k:
                       e.tensor_sub(xn[:, k * 1024:(k + 1) * 1024], x2[b], b_mr))
                sv = op("scalar", lambda e, b=b, k=k: e.activation(
                    sq2[b][:, :], xn[:, k * 1024:(k + 1) * 1024], AF.Square),
                    wait=[("sV", v), ("sT", tdone[b])])
                for h in range(2):
                    mm(psY[:, h * 512:(h + 1) * 512], c_o128[:, :],
                       sq2[b][:, h * 512:(h + 1) * 512], start=(k == 0), stop=(k == KC - 1),
                       wait=[("sA", sv)] if h == 0 else [])
                    tdone[b] = mm(psY[:, 1024 + h * 512:1024 + (h + 1) * 512], c_o128[:, :],
                       xn[:, k * 1024 + h * 512:k * 1024 + (h + 1) * 512],
                       start=(k == 0), stop=(k == KC - 1),
                       wait=([("sV", v)] if h == 0 else []), inc=(h == 1))
            tstat = tdone[(KC - 1) % 2]
            v2 = op("vector", lambda e: e.tensor_copy(b_rs1, psY[:, 0:1024]),
                    wait=[("sT", tstat)])
            a = op("scalar", lambda e: e.activation(b_rs1, b_rs1, AF.Ln,
                    bias=EPS, scale=1.0 / 4096), wait=[("sV", v2)])
            a = op("scalar", lambda e: e.activation(b_rs1, b_rs1, AF.Exp, scale=-0.5),
                   selfwait=True)
            v = op("vector", lambda e: e.tensor_copy(b_cols, psY[:, 1024:2048]),
                   wait=[("sT", tstat)])
            barrier()

            if stop_after is not None and stop_after <= 2:
                raise _StopPlan()
            qld = [0, 0]
            vext = [0, 0]
            vcp = [0, 0]
            tmm = [0, 0]
            vev = [0, 0]
            fdma = [0, 0]
            for mi in range(KC):
                b = mi % 2
                qld[b] = dma("gpsimd", qx[b], qexp[mi], wait=[("sV", vext[b])])
                vext[b] = op("vector", lambda e, b=b: e.tensor_scalar(
                    out=qx[b], in0=qx[b], scalar1=c_sh[:, 0:1], scalar2=63,
                    op0=ALU.logical_shift_right, op1=ALU.bitwise_and),
                    wait=[("dG", qld[b]), ("sV", vcp[b])])
                vcp[b] = op("vector", lambda e, b=b: e.tensor_copy(vb[b][:, :], qx[b]),
                            wait=[("sT", tmm[b])])
                ps = psA if b == 0 else psB
                for kc in range(KC):
                    for h in range(2):
                        tmm[b] = mm(ps[:, h * 512:(h + 1) * 512],
                           vb[b][:, kc * 128:(kc + 1) * 128],
                           xn[:, kc * 1024 + h * 512:kc * 1024 + (h + 1) * 512],
                           start=(kc == 0), stop=(kc == KC - 1),
                           wait=([("sV", vcp[b]), ("sV", vev[b])] if kc == 0 and h == 0 else []),
                           inc=(kc == KC - 1 and h == 1))
                v = op("vector", lambda e, ps=ps, b=b, mi=mi: e.scalar_tensor_tensor(
                    out=ffn2[b], in0=ps[:, :], scalar=c_sc[:, mi:mi + 1],
                    in1=c_fb[:, mi:mi + 1].broadcast_to([128, 1024]),
                    op0=ALU.mult, op1=ALU.add),
                    wait=[("sT", tmm[b]), ("dS", fdma[b])])
                vev[b] = op("vector", lambda e, b=b, mi=mi: e.scalar_tensor_tensor(
                    out=ffn2[b], in0=b_cols, scalar=c_nszp[:, mi:mi + 1],
                    in1=ffn2[b], op0=ALU.mult, op1=ALU.add))
                fdma[b] = dma("sync", ffn_dr[mi * 128:(mi + 1) * 128, :], ffn2[b],
                              wait=[("sV", vev[b])])
            barrier()

            if stop_after is not None and stop_after <= 3:
                raise _StopPlan()
            for k in range(KC):
                op("vector", lambda e, k=k: e.scalar_tensor_tensor(
                    out=xn[:, k * 1024:(k + 1) * 1024],
                    in0=xn[:, k * 1024:(k + 1) * 1024],
                    scalar=c_lnw[:, k:k + 1], in1=b_rs1, op0=ALU.mult, op1=ALU.mult))
            x1 = xn
            barrier()

            if stop_after is not None and stop_after <= 4:
                raise _StopPlan()
            wld = [0, 0]
            wld4 = [0, 0, 0, 0]
            twf = [0, 0, 0, 0]
            wfree = [0, 0]
            psfree = [0, 0]
            dgld = [0, 0]
            convdone = 0
            last_silu = 0
            silu_x2 = [0, 0]
            xsdma = [0, 0]
            vlast_bc = 0
            for j, mi in enumerate(range(32, 67)):
                b = j % 2
                w4 = j % 4
                wld4[w4] = dma("gpsimd", wb4[w4][:, :], w_in[mi],
                               wait=[("sT", twf[w4])])
                ti = mi - 32
                ps = psA if b == 0 else psB
                for kc in range(KC):
                    for h in range(2):
                        wfree[b] = mm(ps[:, h * 512:(h + 1) * 512],
                           wb4[w4][:, kc * 128:(kc + 1) * 128],
                           x1[:, kc * 1024 + h * 512:kc * 1024 + (h + 1) * 512],
                           start=(kc == 0), stop=(kc == KC - 1),
                           wait=([("dG", wld4[w4]), ("sV", psfree[b])] if kc == 0 and h == 0 else []),
                           inc=(kc == KC - 1 and h == 1))
                twf[w4] = wfree[b]
                if mi < 66:
                    ev = op("vector", lambda e, ps=ps, b=b, mi=mi: e.tensor_tensor(
                        xbc2[b][:, :], ps[:, :],
                        c_mup[:, mi:mi + 1].broadcast_to([128, 1024]), ALU.mult),
                        wait=[("sT", wfree[b])])
                    psfree[b] = ev
                    op("vector", lambda e, b=b, ti=ti: e.tensor_copy(
                        edge[:, ti * 3:(ti + 1) * 3], xbc2[b][:, 0:3]))
                    vlast_bc = op("vector", lambda e, b=b, ti=ti: e.tensor_copy(
                        sendb[:, ti * 3:(ti + 1) * 3], xbc2[b][:, 1021:1024]))
                    dgld[b] = dma("gpsimd", dg2[b][:, :], convd[ti],
                                  wait=[("sT", convdone)])
                    for k in range(4):
                        mm(psY[:, 0:512], dg2[b][:, k * 128:(k + 1) * 128],
                           xbc2[b][:, k:k + 512], start=(k == 0), stop=(k == 3),
                           wait=[("dG", dgld[b]), ("sV", ev), ("sV", last_silu)]
                           if k == 0 else [])
                        convdone = mm(psY[:, 512:1021], dg2[b][:, k * 128:(k + 1) * 128],
                           xbc2[b][:, 512 + k:1021 + k], start=(k == 0), stop=(k == 3),
                           inc=(k == 3))
                    last_silu = op("vector", lambda e, b=b: e.tensor_copy(
                        x2[b][:, 0:1021], psY[:, 0:1021]),
                        wait=[("sT", convdone), ("sA", silu_x2[b])])
                    sv = op("scalar", lambda e, b=b: e.activation(
                        xsst[b][:, 0:1021], x2[b][:, 0:1021], AF.Silu),
                        wait=[("sV", last_silu), ("dS", xsdma[b])])
                    silu_x2[b] = sv
                    if ti < 32:
                        xsdma[b] = dma("sync", xs_plain[ti * 128:(ti + 1) * 128, 3:1024],
                                       xsst[b][:, 0:1021], wait=[("sA", sv)])
                    elif ti == 32:
                        op("vector", lambda e, b=b: e.tensor_copy(
                            B_fm[:, 3:1024], xsst[b][:, 0:1021]), wait=[("sA", sv)])
                    else:
                        op("vector", lambda e, b=b: e.tensor_copy(
                            C_fm[:, 3:1024], xsst[b][:, 0:1021]), wait=[("sA", sv)])
                else:
                    ev = op("vector", lambda e, ps=ps, b=b: e.tensor_tensor(
                        x2[b][:, :], ps[:, :],
                        c_mup[:, 66:67].broadcast_to([128, 1024]), ALU.mult),
                        wait=[("sT", wfree[b])])
                    psfree[b] = ev
                    op("vector", lambda e, b=b: e.tensor_copy(
                        dt_fm, x2[b][0:64, :]))
            d_snd = dma("sync", cc1i.ap(), sendb[:, :], wait=[("sV", vlast_bc)])
            cc1done = coll("AllGather", [cc1i.ap()], [cc1o.ap()], wait=[("dS", d_snd)])

            if stop_after is not None and stop_after <= 5:
                raise _StopPlan()
            zev = [0, 0]
            zfree = [0, 0]
            zdma = [0, 0]
            for mi in range(32):
                b = mi % 2
                w4 = (35 + mi) % 4
                wld4[w4] = dma("gpsimd", wb4[w4][:, :], w_in[mi],
                               wait=[("sT", twf[w4])])
                ps = psA if b == 0 else psB
                for kc in range(KC):
                    for h in range(2):
                        wfree[b] = mm(ps[:, h * 512:(h + 1) * 512],
                           wb4[w4][:, kc * 128:(kc + 1) * 128],
                           x1[:, kc * 1024 + h * 512:kc * 1024 + (h + 1) * 512],
                           start=(kc == 0), stop=(kc == KC - 1),
                           wait=([("dG", wld4[w4]), ("sV", psfree[b]), ("sV", zfree[b])]
                           if kc == 0 and h == 0 else []),
                           inc=(kc == KC - 1 and h == 1))
                twf[w4] = wfree[b]
                psfree[b] = 0
                zfree[b] = op("vector", lambda e, ps=ps, b=b, mi=mi: e.tensor_tensor(
                    x2[b][:, :], ps[:, :],
                    c_mup[:, mi:mi + 1].broadcast_to([128, 1024]), ALU.mult),
                    wait=[("sT", wfree[b]), ("sA", zev[b])])
                zev[b] = op("scalar", lambda e, b=b: e.activation(
                    zst[b][:, :], x2[b][:, :], AF.Silu),
                    wait=[("sV", zfree[b]), ("dS", zdma[b])])
                zdma[b] = dma("sync", z_plain[mi * 128:(mi + 1) * 128, :], zst[b][:, :],
                              wait=[("sA", zev[b])])
                if mi == 2:
                    gld = dma("gpsimd", gth[:, :], cc1o.ap()[0:128, :],
                              wait=[("sC", cc1done)])
                    op("vector", lambda e: e.tensor_scalar(
                        out=gth[:, :], in0=gth[:, :], scalar1=c_sel[:, 0:1],
                        scalar2=None, op0=ALU.mult), wait=[("dG", gld)])
                    edma = 0
                    for ti in range(34):
                        op("vector", lambda e, ti=ti: e.tensor_copy(
                            xpe[:, 0:3], gth[:, ti * 3:(ti + 1) * 3]),
                            wait=[("dS", edma)])
                        op("vector", lambda e, ti=ti: e.tensor_copy(
                            xpe[:, 3:6], edge[:, ti * 3:(ti + 1) * 3]))
                        for o_col, base in ((6, 0), (7, 1), (2, 2)):
                            op("vector", lambda e, ti=ti, o=o_col, base=base:
                               e.tensor_scalar(
                                   out=xpe[:, o:o + 1], in0=xpe[:, base:base + 1],
                                   scalar1=c_cw[:, ti * 4:ti * 4 + 1], scalar2=None,
                                   op0=ALU.mult))
                            for kk in range(1, 4):
                                op("vector", lambda e, ti=ti, o=o_col, base=base, kk=kk:
                                   e.scalar_tensor_tensor(
                                       out=xpe[:, o:o + 1],
                                       in0=xpe[:, base + kk:base + kk + 1],
                                       scalar=c_cw[:, ti * 4 + kk:ti * 4 + kk + 1],
                                       in1=xpe[:, o:o + 1], op0=ALU.mult, op1=ALU.add))
                        v = op("vector", lambda e: e.tensor_copy(xpe[:, 0:2], xpe[:, 6:8]))
                        a = op("scalar", lambda e: e.activation(
                            xpe[:, 3:6], xpe[:, 0:3], AF.Silu), wait=[("sV", v)])
                        if ti < 32:
                            edma = dma("sync", xs_plain[ti * 128:(ti + 1) * 128, 0:3],
                                       xpe[:, 3:6], wait=[("sA", a)])
                        elif ti == 32:
                            op("vector", lambda e: e.tensor_copy(B_fm[:, 0:3], xpe[:, 3:6]),
                               wait=[("sA", a)])
                        else:
                            op("vector", lambda e: e.tensor_copy(C_fm[:, 0:3], xpe[:, 3:6]),
                               wait=[("sA", a)])
            barrier()

            if stop_after is not None and stop_after <= 6:
                raise _StopPlan()
            v = op("vector", lambda e: e.tensor_scalar(out=dtv, in0=dt_fm,
                    scalar1=c_dtb[:, 0:1], scalar2=None, op0=ALU.add))
            v = op("vector", lambda e: e.tensor_scalar(out=t2d, in0=dtv,
                    scalar1=-1.0, scalar2=None, op0=ALU.mult))
            v = op("vector", lambda e: e.tensor_max(t1, dtv, t2d))
            a = op("scalar", lambda e: e.activation(t1, t1, AF.Exp, scale=-1.0),
                   wait=[("sV", v)])
            aT = op("scalar", lambda e: e.activation(t1, t1, AF.Ln, bias=1.0),
                    selfwait=True)
            v = op("vector", lambda e: e.tensor_scalar(out=dt_soft, in0=dtv, scalar1=0.0,
                    scalar2=None, op0=ALU.max))
            vdt = op("vector", lambda e: e.tensor_add(dt_soft, dt_soft, t1),
                     wait=[("sA", aT)])
            aA = op("scalar", lambda e: e.activation(nA, c_al[:, 0:1], AF.Exp),
                    wait=[("sV", vdt)])
            vA = op("vector", lambda e: e.tensor_scalar(out=nA, in0=nA,
                    scalar1=-1.0, scalar2=None, op0=ALU.mult), wait=[("sA", aA)])
            vlog = op("vector", lambda e: e.tensor_scalar(out=logdA, in0=dt_soft,
                    scalar1=nA[:, 0:1], scalar2=None, op0=ALU.mult), wait=[("sV", vA)])
            for c in range(NCH):
                vscan = op("vector", lambda e, c=c: e.tensor_tensor_scan(
                    s_all[:, c * 128:(c + 1) * 128], logdA[:, c * 128:(c + 1) * 128],
                    z64[:, :], 0.0, ALU.add, ALU.add))
            v = op("vector", lambda e: e.tensor_copy(stot, s_all[:, 127:1024:128]),
                   selfwait=True)
            v = op("vector", lambda e: e.tensor_copy(seg[:, 7:8], stot[:, 7:8]),
                   selfwait=True)
            for c in range(6, -1, -1):
                v = op("vector", lambda e, c=c: e.tensor_add(
                    seg[:, c:c + 1], stot[:, c:c + 1], seg[:, c + 1:c + 2]),
                    selfwait=True)
            vseg = v
            op("scalar", lambda e: e.activation(lam_a, stot, AF.Exp), wait=[("sV", vseg)])
            tprev = [0, 0]
            vtr = 0
            for c in range(NCH):
                t = op("tensor", lambda e, c=c: e.transpose(
                    psAv64[0], s_all[:, c * 128:(c + 1) * 128], c_id[0:64, 0:64]),
                    wait=[("sV", max(vscan, tprev[0]))], inc=True)
                tprev[0] = op("vector", lambda e, c=c: e.tensor_copy(
                    s_tok[:, c * 64:(c + 1) * 64], psAv64[0]), wait=[("sT", t)])
                t = op("tensor", lambda e, c=c: e.transpose(
                    psAv64[1], dt_soft[:, c * 128:(c + 1) * 128], c_id[0:64, 0:64]),
                    wait=[("sV", max(vdt, tprev[1]))], inc=True)
                tprev[1] = op("vector", lambda e, c=c: e.tensor_copy(
                    dt_tok[:, c * 64:(c + 1) * 64], psAv64[1]), wait=[("sT", t)])
                vtr = tprev[1]
            d_f = dma("sync", rows_d[0:1, 4096:4608], stot, wait=[("sV", vseg)],
                      allow_slow_non_contiguous=True)
            dma("sync", b_stot, rows_d[0:1, 4096:4608].partition_broadcast(128),
                wait=[("dS", d_f)])
            for c in range(NCH):
                dmat("sync", B_tok[:, c * 128:(c + 1) * 128],
                     B_fm[:, c * 128:(c + 1) * 128])
            barrier()

            if stop_after is not None and stop_after <= 7:
                raise _StopPlan()
            v_bsend = 0
            a_we = 0
            v_wd = 0
            t_wdfree = 0
            for c in range(NCH):
                d_f = dma("sync", rows_d[0:1, 4608 + c * 64:4672 + c * 64],
                          seg[:, c:c + 1], wait=[("sV", v_bsend)],
                          allow_slow_non_contiguous=True)
                d_b = dma("sync", b_send, rows_d[0:1, 4608 + c * 64:4672 + c * 64]
                          .partition_broadcast(128), wait=[("dS", d_f)])
                d_x = dmat("sync", xst[:, :], xs_plain[:, c * 128:(c + 1) * 128],
                           wait=[("sV", v_wd)])
                v_bsend = op("vector", lambda e, c=c: e.tensor_sub(
                    d1, b_send, s_tok[:, c * 64:(c + 1) * 64]),
                    wait=[("dS", d_b), ("sA", a_we)])
                a_we = op("scalar", lambda e: e.activation(we_t, d1, AF.Exp),
                          wait=[("sV", v_bsend)])
                v = op("vector", lambda e, c=c: e.tensor_mul(
                    wdt_b[:, :], we_t, dt_tok[:, c * 64:(c + 1) * 64]),
                    wait=[("sA", a_we)])
                v_wd = op("vector", lambda e: e.tensor_tensor(
                    WD[:, :].rearrange("p (r q) -> p r q", r=64),
                    xst[:, :].rearrange("p (r q) -> p r q", r=64),
                    wdt_b[:, :, None].broadcast_to([128, 64, 64]), ALU.mult),
                    wait=[("sV", v), ("dS", d_x), ("sT", t_wdfree)])
                for q in range(8):
                    if q < 2:
                        o = psA[:, q * 512:(q + 1) * 512]
                    elif q < 4:
                        o = psB[:, (q - 2) * 512:(q - 1) * 512]
                    else:
                        o = psY[:, (q - 4) * 512:(q - 3) * 512]
                    t_wdfree = mm(o, B_tok[:, c * 128:(c + 1) * 128],
                       WD[:, q * 512:(q + 1) * 512],
                       start=(c == 0), stop=(c == NCH - 1),
                       wait=([("sV", v_wd)] if q == 0 else []), inc=(q == 7))
            v = op("vector", lambda e: e.tensor_copy(H_f[:, 0:1024], psA[:, :]),
                   wait=[("sT", t_wdfree)])
            v = op("vector", lambda e: e.tensor_copy(H_f[:, 1024:2048], psB[:, :]))
            vHev = op("vector", lambda e: e.tensor_copy(H_f[:, 2048:4096], psY[:, :]))
            d_h = dma("sync", cc2i.ap(), H_f[:, :], wait=[("sV", vHev)])
            cc2done = coll("AllGather", [cc2i.ap()], [cc2o.ap()], wait=[("dS", d_h)])

            if stop_after is not None and stop_after <= 8:
                raise _StopPlan()
            vprev = vHev
            for c in range(NCH):
                t = op("tensor", lambda e, c=c: e.matmul(
                    psA[:, 0:128], B_fm[:, c * 128:(c + 1) * 128],
                    C_fm[:, c * 128:(c + 1) * 128], start=True, stop=True,
                    skip_group_check=True),
                    wait=[("sV", vprev)], inc=True)
                vprev = op("vector", lambda e, c=c: e.tensor_mul(
                    CBm[:, c * 128:(c + 1) * 128], psA[:, 0:128], c_tri[:, :]),
                    wait=[("sT", t)])
            dma("gpsimd", ind_sb[:, :], c_ind_d)
            gld = dma("gpsimd", srow[:, :], cc2o.ap()[0:128, :], wait=[("sC", cc2done)])
            vHi = op("vector", lambda e: e.tensor_scalar(
                out=H_f[:, :], in0=srow[:, :], scalar1=c_sel[:, 0:1], scalar2=None,
                op0=ALU.mult), wait=[("dG", gld)])
            vHbf = op("vector", lambda e: e.tensor_copy(Hbf[:, :], H_f[:, :]))
            barrier()

            if stop_after is not None and stop_after <= 9:
                raise _StopPlan()
            d_f = dma("sync", srow_d[0:1, :], s_all[0:32, 0:128])
            dsr = dma("sync", srow[:, :], srow_d.partition_broadcast(128),
                      wait=[("dS", d_f)])
            dszt = dmat("sync", szt[:, :], z_plain[:, 0:128])
            dxt = dmat("sync", xst[:, :], xs_plain[:, 0:128])
            dygw = 0
            tlam = 0
            v_t2 = 0
            v_y1ev = 0
            a_we = 0
            vg = 0
            tY_last = 0
            t_S_last = 0
            for c in range(NCH):
                v_stot = op("vector", lambda e, c=c: e.tensor_sub(
                    d1, b_stot[:, c:512:8], s_tok[:, c * 64:(c + 1) * 64]),
                    wait=[("sA", a_we)])
                a_we = op("scalar", lambda e: e.activation(we_t, d1, AF.Exp),
                          wait=[("sV", v_stot)])
                a_es = op("scalar", lambda e, c=c: e.activation(
                    es_t, s_tok[:, c * 64:(c + 1) * 64], AF.Exp), wait=[("sV", v_t2)])
                for h in range(2):
                    vD = 0
                    for jj in range(32):
                        r = 32 * h + jj
                        vD = op("vector", lambda e, jj=jj, r=r, c=c: e.tensor_scalar(
                            out=srow[:, jj * 128:(jj + 1) * 128],
                            in0=srow[:, jj * 128:(jj + 1) * 128],
                            scalar1=s_tok[:, c * 64 + r:c * 64 + r + 1], scalar2=0.0,
                            op0=ALU.subtract, op1=ALU.min),
                            wait=[("dS", dsr)] if jj == 0 else [])
                    aE = op("scalar", lambda e: e.activation(
                        E_sb[:, :], srow[:, :], AF.Exp),
                        wait=[("sV", vD), ("sT", tY_last), ("dS", dygw)])
                    if h == 0:
                        d_f = dma("sync", srow_d[0:1, :],
                                  s_all[32:64, c * 128:(c + 1) * 128], wait=[("sA", aE)])
                        dsr = dma("sync", srow[:, :], srow_d.partition_broadcast(128),
                                  wait=[("dS", d_f)])
                    elif c < NCH - 1:
                        d_f = dma("sync", srow_d[0:1, :],
                                  s_all[0:32, (c + 1) * 128:(c + 2) * 128],
                                  wait=[("sA", aE)])
                        dsr = dma("sync", srow[:, :], srow_d.partition_broadcast(128),
                                  wait=[("dS", d_f)])
                    vM = op("vector", lambda e, c=c: e.tensor_tensor(
                        E_sb[:, :].rearrange("p (r i) -> p r i", r=32),
                        E_sb[:, :].rearrange("p (r i) -> p r i", r=32),
                        CBm[:, c * 128:(c + 1) * 128][:, None, :]
                        .broadcast_to([128, 32, 128]), ALU.mult), wait=[("sA", aE)])
                    vX = op("vector", lambda e, c=c, h=h: e.tensor_tensor(
                        XD[:, h * 2048:(h + 1) * 2048].rearrange("p (r q) -> p r q", r=32),
                        xst[:, h * 2048:(h + 1) * 2048].rearrange("p (r q) -> p r q", r=32),
                        dt_tok[:, c * 64 + 32 * h:c * 64 + 32 * h + 32][:, :, None]
                        .broadcast_to([128, 32, 64]), ALU.mult),
                        wait=([("dS", dxt)] if h == 0 else []) +
                        [("sV", v_y1ev), ("sT", t_S_last)])
                    for q in range(4):
                        o = psA[:, q * 512:(q + 1) * 512] if q < 2 else                         psB[:, (q - 2) * 512:(q - 1) * 512]
                        mm(o, C_fm[:, c * 128:(c + 1) * 128],
                           Hbf[:, h * 2048 + q * 512:h * 2048 + (q + 1) * 512],
                           start=True, stop=True,
                           wait=[("sV", v_t2), ("sV", vHbf)] if q == 0 else [])
                    for jl in range(32):
                        r = h * 32 + jl
                        tY = mm(psY[:, jl * 64:(jl + 1) * 64],
                           E_sb[:, jl * 128:(jl + 1) * 128],
                           XD[:, r * 64:(r + 1) * 64],
                           start=True, stop=True,
                           wait=([("sV", vM), ("sV", vX), ("sV", v_y1ev)]
                           if jl == 0 else []), inc=(jl == 31))
                    tY_last = tY
                    v_dxs = op("vector", lambda e, h=h: e.tensor_tensor(
                        XW[:, h * 2048:(h + 1) * 2048], xst[:, h * 2048:(h + 1) * 2048],
                        Db[:, h * 2048:(h + 1) * 2048], ALU.mult), wait=[("sT", tY)])
                    v_y1ev = op("vector", lambda e, h=h: e.tensor_add(
                        y_half[:, :], XW[:, h * 2048:(h + 1) * 2048], psY[:, 0:2048]),
                        wait=[("sV", v_dxs)])
                    v = op("vector", lambda e, h=h: e.tensor_tensor(
                        Htmp[:, :], psA[:, :],
                        es_t[:, h * 32:h * 32 + 16][:, :, None]
                        .broadcast_to([128, 16, 64]), ALU.mult),
                        wait=[("sA", a_es), ("sV", v_y1ev)])
                    v = op("vector", lambda e: e.tensor_add(
                        y_half[:, 0:1024], y_half[:, 0:1024], Htmp[:, :]))
                    v = op("vector", lambda e, h=h: e.tensor_tensor(
                        Htmp[:, :], psB[:, :],
                        es_t[:, h * 32 + 16:h * 32 + 32][:, :, None]
                        .broadcast_to([128, 16, 64]), ALU.mult))
                    v_t2 = op("vector", lambda e: e.tensor_add(
                        y_half[:, 1024:2048], y_half[:, 1024:2048], Htmp[:, :]))
                    vg = op("vector", lambda e, h=h: e.tensor_tensor(
                        yg[:, h * 2048:(h + 1) * 2048], y_half[:, :],
                        szt[:, h * 2048:(h + 1) * 2048], ALU.mult),
                        wait=[("dS", dszt)])
                aS = op("scalar", lambda e: e.activation(
                    E_sb[:, :], yg[:, :], AF.Square),
                    wait=[("sV", vg), ("sT", tY_last)])
                v_ms = op("vector", lambda e: e.reduce_sum(
                    ms_col, E_sb[:, :], mybir.AxisListType.X), wait=[("sA", aS)])
                a = op("scalar", lambda e: e.activation(rs_col, ms_col, AF.Ln,
                        bias=EPS, scale=1.0 / 4096), wait=[("sV", v_ms)])
                a_rs = op("scalar", lambda e: e.activation(rs_col, rs_col, AF.Exp,
                        scale=-0.5), selfwait=True)
                v_w = op("vector", lambda e: e.tensor_scalar(
                    out=E_sb[:, :], in0=yg[:, :], scalar1=rs_col, scalar2=None,
                    op0=ALU.mult), wait=[("sA", a_rs)])
                dygw = dma("sync", ygw_dr[c * 128:(c + 1) * 128, :], E_sb[:, :],
                           wait=[("sV", v_w)])
                v_wdt = op("vector", lambda e, c=c: e.tensor_mul(
                    wdt_b[:, :], we_t, dt_tok[:, c * 64:(c + 1) * 64]),
                    wait=[("sA", a_we)])
                vWD = op("vector", lambda e: e.tensor_tensor(
                    WD[:, :].rearrange("p (r q) -> p r q", r=64),
                    xst[:, :].rearrange("p (r q) -> p r q", r=64),
                    wdt_b[:, :, None].broadcast_to([128, 64, 64]), ALU.mult),
                    wait=[("sV", v_wdt), ("sV", v_y1ev)])
                vL = op("vector", lambda e, c=c: e.tensor_scalar(
                    out=lamT[:, :], in0=c_ones64[:, :], scalar1=lam_a[:, c:c + 1],
                    scalar2=None, op0=ALU.mult), wait=[("sT", tlam)])
                vHq = 0
                for q in range(4):
                    for hh in range(2):
                        tlam = mm(psB[:, hh * 512:(hh + 1) * 512], lamT[:, :],
                                  ind_sb[:, q * 1024 + hh * 512:q * 1024 + (hh + 1) * 512],
                                  start=True, stop=(hh == 1),
                                  wait=([("sV", vL), ("sV", vHq)] +
                                        ([("sV", v_t2)] if q == 0 else []))
                                  if hh == 0 else [],
                                  inc=(hh == 1))
                    vt = op("vector", lambda e, q=q: e.tensor_tensor(
                        Htmp[:, :], psB[:, :], H_f[:, q * 1024:(q + 1) * 1024], ALU.mult),
                        wait=[("sT", tlam)])
                    for hh in range(2):
                        tS = mm(psA[:, hh * 512:(hh + 1) * 512],
                                B_tok[:, c * 128:(c + 1) * 128],
                                WD[:, q * 1024 + hh * 512:q * 1024 + (hh + 1) * 512],
                                start=True, stop=(hh == 1),
                                wait=([("sV", vWD), ("sV", vHq)] +
                                      ([("sV", v_t2)] if q == 0 else []))
                                if hh == 0 else [],
                                inc=(hh == 1))
                    vHq = op("vector", lambda e, q=q: e.tensor_add(
                        H_f[:, q * 1024:(q + 1) * 1024], Htmp[:, :], psA[:, :]),
                        wait=[("sT", tS), ("sV", vt)])
                    t_S_last = tS
                vHbf = op("vector", lambda e: e.tensor_copy(Hbf[:, :], H_f[:, :]),
                          wait=[("sV", vHq)])
                if c < NCH - 1:
                    dxt = dmat("sync", xst[:, :],
                               xs_plain[:, (c + 1) * 128:(c + 2) * 128],
                               wait=[("sV", vWD)])
                    dszt = dmat("sync", szt[:, :],
                                z_plain[:, (c + 1) * 128:(c + 2) * 128],
                                wait=[("sV", vg)])
            barrier()
            if debug:
                dma("sync", d_s, s_all)
                dma("sync", d_dt, dt_fm)
                dma("sync", d_sm[:, 0:8], stot)
                dma("sync", d_sm[:, 8:16], seg)
                dma("sync", d_sm[:, 16:24], lam_a)
                dma("sync", d_sm[:, 24:25], nA, allow_slow_non_contiguous=True)
                dma("sync", d_ds, dt_soft)
                dma("sync", d_ld, logdA)
                dma("sync", d_H, cc2i.ap())
                dma("sync", d_Hg, cc2o.ap())
                dma("sync", d_ygw, ygw_dr)
                dma("sync", d_xs, xs_plain)
                dma("sync", d_z, z_plain)
                dma("sync", d_ffn, ffn_dr)
                barrier()

            if stop_after is not None and stop_after <= 10:
                raise _StopPlan()
            dyl = 0
            for kc in range(KC):
                dyl = dmat("sync", AB1[:, kc * 1024:(kc + 1) * 1024],
                           ygw_dr[:, kc * 128:(kc + 1) * 128])
            wfree = [0, 0]
            vsc = [0, 0]
            vev = [0, 0]
            odma = [0, 0]
            twf = [0, 0, 0, 0]
            wld4 = [0, 0, 0, 0]
            for mi in range(KC):
                b = mi % 2
                w4 = mi % 4
                wld4[w4] = dma("gpsimd", wb4[w4][:, :], w_out[mi],
                               wait=[("sT", twf[w4])])
                dx = dma("sync", x2[b], xin[mi * 128:(mi + 1) * 128, :],
                         wait=[("sV", vev[b])])
                df = dma("sync", ffn2[b], ffn_dr[mi * 128:(mi + 1) * 128, :])
                vsc[b] = op("vector", lambda e, w4=w4: e.tensor_tensor(
                    wb4[w4][:, :].rearrange("p (kc m) -> p kc m", kc=KC),
                    wb4[w4][:, :].rearrange("p (kc m) -> p kc m", kc=KC),
                    c_nw[:, :, None].broadcast_to([128, KC, 128]), ALU.mult),
                    wait=[("dG", wld4[w4])])
                ps = psA if b == 0 else psB
                for kc in range(KC):
                    for th in range(2):
                        wfree[b] = mm(ps[:, th * 512:(th + 1) * 512],
                           wb4[w4][:, kc * 128:(kc + 1) * 128],
                           AB1[:, kc * 1024 + th * 512:kc * 1024 + (th + 1) * 512],
                           start=(kc == 0), stop=(kc == KC - 1),
                           wait=([("sV", vsc[b]), ("dS", dyl), ("sV", vev[b])]
                           if kc == 0 and th == 0 else []),
                           inc=(kc == KC - 1 and th == 1))
                twf[w4] = wfree[b]
                v = op("vector", lambda e, ps=ps, b=b: e.tensor_add(o2[b], ps[:, :], x2[b]),
                       wait=[("sT", wfree[b]), ("dS", df), ("dS", odma[b])])
                vev[b] = op("vector", lambda e, b=b: e.tensor_add(o2[b], o2[b], ffn2[b]))
                odma[b] = dma("sync", out_fm[mi * 128:(mi + 1) * 128, :], o2[b],
                              wait=[("sV", vev[b])])

        except _StopPlan:
            pass
        _w("sync", [("dS", C["dS"])])
        _w("gpsimd", [("dG", C["dG"])])

        block = En(nc.Block())

        @block.sync
        def _(e):
            for f in prog["sync"]:
                f(e)

        @block.scalar
        def _(e):
            for f in prog["scalar"]:
                f(e)

        @block.vector
        def _(e):
            for f in prog["vector"]:
                f(e)

        @block.tensor
        def _(e):
            for f in prog["tensor"]:
                f(e)

        @block.gpsimd
        def _(e):
            for f in prog["gpsimd"]:
                f(e)

    return nc


def _host_prep(inputs):
    f32 = np.float32
    W = np.asarray(inputs["L0_attn_in_proj_w"], f32)
    Wp = np.concatenate([W, np.zeros((4096, 64), f32)], axis=1)
    w_in = np.ascontiguousarray(Wp.reshape(32, 128, MC, 128).transpose(2, 1, 0, 3))
    Wo = np.asarray(inputs["L0_attn_out_proj_w"], f32)
    w_out = np.ascontiguousarray(Wo.reshape(32, 128, 32, 128).transpose(2, 1, 0, 3))
    q = np.asarray(inputs["L0_ffn_qweight"], np.int32)
    k = np.arange(4096)
    src = 3 * (k // 4) + np.maximum(k % 4 - 1, 0)
    qe = np.ascontiguousarray(
        q[:, src].T.reshape(32, 128, 32, 128).transpose(2, 1, 0, 3))
    cw = np.asarray(inputs["L0_attn_conv_w"], f32)
    convd = np.zeros((34, 128, 4, 128), f32)
    ii = np.arange(128)
    for ti in range(34):
        for kk in range(4):
            convd[ti, ii, kk, ii] = cw[ti * 128:(ti + 1) * 128, kk]
    convd = convd.astype(NBF)
    mup = np.asarray(inputs["L0_attn_mup_vector"], f32)
    mupp = np.concatenate([mup, np.zeros(64, f32)])
    c_mup = np.ascontiguousarray(mupp.reshape(MC, 128).T)
    sc = np.asarray(inputs["L0_ffn_weight_scale"], f32).reshape(4096)
    zp = np.asarray(inputs["L0_ffn_weight_zero_point"], f32).reshape(4096)
    fb = np.asarray(inputs["L0_ffn_bias"], f32)
    c_sc = np.ascontiguousarray(sc.reshape(32, 128).T)
    c_nszp = np.ascontiguousarray((-sc * zp).reshape(32, 128).T)
    c_fb = np.ascontiguousarray(fb.reshape(32, 128).T)
    D = np.asarray(inputs["L0_attn_D"], f32)
    c_df = np.repeat(D, 64).reshape(1, 4096).astype(NBF)
    nw = np.asarray(inputs["L0_attn_norm_w"], f32)
    c_nw = np.ascontiguousarray(nw.reshape(32, 128).T)
    lnw = np.asarray(inputs["L0_attn_ln_w"], f32)
    c_lnw = np.ascontiguousarray(lnw.reshape(32, 128).T)
    c_cw = np.zeros((128, 136), f32)
    for ti in range(34):
        c_cw[:, ti * 4:(ti + 1) * 4] = cw[ti * 128:(ti + 1) * 128, :]
    c_sh = ((np.arange(128) % 4 == 1) * 6).astype(np.int32).reshape(128, 1)
    c_dtb = np.asarray(inputs["L0_attn_dt_bias"], f32).reshape(64, 1)
    c_al = np.asarray(inputs["L0_attn_A_log"], f32).reshape(64, 1)
    c_id = np.eye(128, dtype=f32)
    jj = np.arange(128)
    c_tri = (jj[:, None] <= jj[None, :]).astype(f32)
    c_ind = np.repeat(np.eye(64, dtype=f32), 64, axis=1).astype(NBF)

    shared = dict(w_in=w_in, w_out=w_out, qexp=qe, convd=convd, c_id=c_id,
                  c_tri=c_tri, c_ind=c_ind, c_mup=c_mup, c_sc=c_sc,
                  c_nszp=c_nszp, c_fb=c_fb, c_df=c_df, c_nw=c_nw, c_lnw=c_lnw,
                  c_cw=c_cw, c_sh=c_sh, c_dtb=c_dtb, c_al=c_al,
                  c_cv=np.array([[EPS, -0.5, -1.0, 1.0, 1.0 / 4096, 0.0, 0.0, 0.0]],
                                np.float32))
    _ = inputs["L0_attn_residual"]
    x = np.asarray(inputs["L0_attn_hidden_states"], np.float32)
    in_maps = []
    for core in range(8):
        b, h = core // 2, core % 2
        xc = np.ascontiguousarray(x[b, h * 1024:(h + 1) * 1024, :].T)
        sel = np.full((128, 1), 1.0 if h == 1 else 0.0, np.float32)
        m = dict(shared)
        m["xin"] = xc
        m["c_sel"] = sel
        in_maps.append(m)
    return in_maps


def kernel(**inputs):
    if "nc" not in _NC_CACHE:
        _NC_CACHE["nc"] = build_nc()
    nc = _NC_CACHE["nc"]
    in_maps = _host_prep(inputs)
    res = run_bass_kernel_spmd(nc, in_maps, core_ids=list(range(8)))
    out = np.empty((4, 2048, 4096), np.float32)
    for core in range(8):
        b, h = core // 2, core % 2
        out[b, h * 1024:(h + 1) * 1024, :] = res.results[core]["out_fm"].T
    return out


# version 09 (final)
# speedup vs baseline: 1.1084x; 1.0049x over previous
import contextlib

import numpy as np
import ml_dtypes

import concourse.bass as bass
import concourse.mybir as mybir
from concourse.bass_utils import run_bass_kernel_spmd

EPS = 1e-5
F32 = mybir.dt.float32
BF16 = mybir.dt.bfloat16
I32 = mybir.dt.int32
AF = mybir.ActivationFunctionType
ALU = mybir.AluOpType
NBF = ml_dtypes.bfloat16

T = 1024
KC = 32
MC = 67
NCH = 8
ENGINES = ("sync", "scalar", "vector", "tensor", "gpsimd")
RG = [[0, 1], [2, 3], [4, 5], [6, 7]]

_NC_CACHE = {}


class _StopPlan(Exception):
    pass


def build_nc(stop_after=None, debug=False):
    nc = bass.Bass("TRN2", target_bir_lowering=False)

    xin = nc.dram_tensor("xin", [4096, T], F32, kind="ExternalInput").ap()
    w_in = nc.dram_tensor("w_in", [MC, 128, KC, 128], F32, kind="ExternalInput").ap()
    w_out = nc.dram_tensor("w_out", [KC, 128, KC, 128], F32, kind="ExternalInput").ap()
    qexp = nc.dram_tensor("qexp", [KC, 128, KC, 128], I32, kind="ExternalInput").ap()
    convd = nc.dram_tensor("convd", [34, 128, 4, 128], BF16, kind="ExternalInput").ap()
    c_id_d = nc.dram_tensor("c_id", [128, 128], F32, kind="ExternalInput").ap()
    c_tri_d = nc.dram_tensor("c_tri", [128, 128], F32, kind="ExternalInput").ap()
    c_ind_d = nc.dram_tensor("c_ind", [64, 4096], BF16, kind="ExternalInput").ap()
    c_mup_d = nc.dram_tensor("c_mup", [128, MC], F32, kind="ExternalInput").ap()
    c_sc_d = nc.dram_tensor("c_sc", [128, KC], F32, kind="ExternalInput").ap()
    c_nszp_d = nc.dram_tensor("c_nszp", [128, KC], F32, kind="ExternalInput").ap()
    c_fb_d = nc.dram_tensor("c_fb", [128, KC], F32, kind="ExternalInput").ap()
    c_df_d = nc.dram_tensor("c_df", [1, 4096], BF16, kind="ExternalInput").ap()
    c_nw_d = nc.dram_tensor("c_nw", [128, KC], F32, kind="ExternalInput").ap()
    c_lnw_d = nc.dram_tensor("c_lnw", [128, KC], F32, kind="ExternalInput").ap()
    c_cw_d = nc.dram_tensor("c_cw", [128, 136], F32, kind="ExternalInput").ap()
    c_sh_d = nc.dram_tensor("c_sh", [128, 1], I32, kind="ExternalInput").ap()
    c_dtb_d = nc.dram_tensor("c_dtb", [64, 1], F32, kind="ExternalInput").ap()
    c_al_d = nc.dram_tensor("c_al", [64, 1], F32, kind="ExternalInput").ap()
    c_sel_d = nc.dram_tensor("c_sel", [128, 1], F32, kind="ExternalInput").ap()
    c_cv_d = nc.dram_tensor("c_cv", [1, 8], F32, kind="ExternalInput").ap()
    out_fm = nc.dram_tensor("out_fm", [4096, T], F32, kind="ExternalOutput").ap()
    if debug:
        d_xs = nc.dram_tensor("d_xs", [4096, T], BF16, kind="ExternalOutput").ap()
        d_z = nc.dram_tensor("d_z", [4096, T], BF16, kind="ExternalOutput").ap()
        d_ffn = nc.dram_tensor("d_ffn", [4096, T], F32, kind="ExternalOutput").ap()
        d_ygw = nc.dram_tensor("d_ygw", [T, 4096], BF16, kind="ExternalOutput").ap()
        d_H = nc.dram_tensor("d_H", [128, 4096], F32, kind="ExternalOutput").ap()
        d_Hg = nc.dram_tensor("d_Hg", [256, 4096], F32, kind="ExternalOutput").ap()
        d_dt = nc.dram_tensor("d_dt", [64, T], F32, kind="ExternalOutput").ap()
        d_s = nc.dram_tensor("d_s", [64, T], F32, kind="ExternalOutput").ap()
        d_sm = nc.dram_tensor("d_sm", [64, 64], F32, kind="ExternalOutput").ap()
        d_ds = nc.dram_tensor("d_ds", [64, T], F32, kind="ExternalOutput").ap()
        d_ld = nc.dram_tensor("d_ld", [64, T], F32, kind="ExternalOutput").ap()

    z_plain = nc.dram_tensor("z_plain", [4096, T], BF16).ap()
    xs_plain = nc.dram_tensor("xs_plain", [4096, T], BF16).ap()
    ffn_dr = nc.dram_tensor("ffn_dr", [4096, T], F32).ap()
    ygw_dr = nc.dram_tensor("ygw_dr", [T, 4096], BF16).ap()
    rows_d = nc.dram_tensor("rows_d", [1, 8192], F32).ap()
    srow_d = nc.dram_tensor("srow_d", [1, 4096], F32).ap()
    cc1i = nc.dram_tensor("cc1i", [128, 102], BF16)
    cc1o = nc.dram_tensor("cc1o", [256, 102], BF16)
    cc2i = nc.dram_tensor("cc2i", [128, 4096], F32)
    cc2o = nc.dram_tensor("cc2o", [256, 4096], F32)

    with contextlib.ExitStack() as es:
        En = es.enter_context

        AB1 = En(nc.sbuf_tensor("AB1", [128, 32768], BF16))
        AB2 = En(nc.sbuf_tensor("AB2", [128, 24576], BF16))
        AB3 = En(nc.sbuf_tensor("AB3", [128, 8192], BF16))
        AB4 = En(nc.sbuf_tensor("AB4", [128, 12288], BF16))
        AFR = En(nc.sbuf_tensor("AFR", [128, 7168], F32))
        c_id = En(nc.sbuf_tensor("s_id", [128, 128], F32))
        c_tri = En(nc.sbuf_tensor("s_tri", [128, 128], F32))
        c_mup = En(nc.sbuf_tensor("s_mup", [128, MC], F32))
        c_sc = En(nc.sbuf_tensor("s_sc", [128, KC], F32))
        c_nszp = En(nc.sbuf_tensor("s_nszp", [128, KC], F32))
        c_fb = En(nc.sbuf_tensor("s_fb", [128, KC], F32))
        Db = En(nc.sbuf_tensor("s_Db", [128, 4096], BF16))
        c_nw = En(nc.sbuf_tensor("s_nw", [128, KC], F32))
        c_lnw = En(nc.sbuf_tensor("s_lnw", [128, KC], F32))
        c_cw = En(nc.sbuf_tensor("s_cw", [128, 136], F32))
        c_sh = En(nc.sbuf_tensor("s_sh", [128, 1], I32))
        c_dtb = En(nc.sbuf_tensor("s_dtb", [64, 1], F32))
        c_al = En(nc.sbuf_tensor("s_al", [64, 1], F32))
        c_sel = En(nc.sbuf_tensor("s_sel", [128, 1], F32))
        c_ones = En(nc.sbuf_tensor("s_ones", [128, 1], BF16))
        c_o128 = En(nc.sbuf_tensor("s_o128", [128, 128], BF16))
        c_ones64 = En(nc.sbuf_tensor("s_ones64", [64, 128], F32))
        z64 = En(nc.sbuf_tensor("s_z64", [64, 128], F32))
        c_cv = En(nc.sbuf_tensor("s_cv", [128, 8], F32))

        psA = En(nc.psum_tensor("psA", [128, 1024], F32))
        psB = En(nc.psum_tensor("psB", [128, 1024], F32))
        psY = En(nc.psum_tensor("psY", [128, 2048], F32))
        psAv64 = [psA[:, 0:64], psB[:, 0:64]]

        sem = {}
        for s in ("dS", "dG", "sT", "sV", "sA", "sP", "sC"):
            sem[s] = En(nc.semaphore(s))

        AB1f = AB1[:, :].bitcast(F32)
        xn = AB1
        srow = AB1f[:, 0:4096]
        H_f = AB1f[:, 4096:8192]
        y_half = AB1f[:, 8192:10240]
        Htmp = AB1f[:, 10240:11264]
        ind_sb = AB3[0:64, 0:4096]

        AB2i = AB2[:, :].bitcast(I32)
        qx = [AB2i[:, 0:4096], AB2i[:, 4096:8192]]
        E_sb = AB2[:, 0:4096]
        XW = AB2[:, 4096:8192]
        XD = XW
        WD = XW
        Hbf = AB2[:, 8192:12288]
        xst = AB2[:, 12288:16384]
        yg = AB2[:, 16384:20480]
        szt = AB2[:, 20480:24576]

        wb = [AB3[:, 0:4096], AB3[:, 4096:8192]]
        vb = wb
        wb4 = [AB3[:, 0:4096], AB3[:, 4096:8192], AB2[:, 0:4096], AB2[:, 4096:8192]]

        B_fm = AB4[:, 0:1024]
        C_fm = AB4[:, 1024:2048]
        B_tok = AB4[:, 2048:3072]
        CBm = AB4[:, 3072:4096]
        dt_tok = AB4[:, 4096:4608]
        xb2 = [AB4[:, 4608:5632], AB4[:, 5632:6656]]
        zst = xb2
        sq2 = [AB4[:, 6656:7680], AB4[:, 7680:8704]]
        xsst = sq2
        xbc2 = [AB4[:, 8704:9728], AB4[:, 9728:10752]]
        dg2 = [AB4[:, 10752:11264], AB4[:, 11264:11776]]
        edge = AB4[:, 11776:11878]
        sendb = AB4[:, 11878:11980]
        xpe = AB4[:, 11980:11988]
        lamT = AB4[0:64, 11988:12116]
        gth = AB4[:, 12116:12218]
        wdt_b = AB4[:, 12218:12282]

        x2 = [AFR[:, 0:1024], AFR[:, 1024:2048]]
        b_rstd = AFR[:, 2048:3072]
        b_mr = AFR[:, 3072:4096]
        ffn2 = [AFR[:, 2048:3072], AFR[:, 3072:4096]]
        r_mean = AFR[0:1, 4096:5120]
        r_tmp = AFR[0:1, 5120:6144]
        r_rstd = AFR[0:1, 6144:7168]
        r_mr = r_mean
        r_rs1 = AFR[0:1, 4096:5120]
        r_cols = AFR[0:1, 6144:7168]
        b_stot = AFR[:, 4096:4608]
        b_send = AFR[:, 4608:4672]
        d1 = AFR[:, 4672:4736]
        we_t = AFR[:, 4736:4800]
        es_t = AFR[:, 4800:4864]
        rs_col = AFR[:, 4864:4865]
        ms_col = AFR[:, 4872:4873]
        o2 = [AFR[:, 4096:5120], AFR[:, 5120:6144]]
        b_rs1 = AFR[:, 6144:7168]
        dt_fm = AFR[0:64, 6144:7168]
        s_all = AB1f[0:64, 11264:12288]
        dtv = AB1f[0:64, 12288:13312]
        dt_soft = dtv
        t1 = AB1f[0:64, 13312:14336]
        logdA = t1
        s_tok = AB1f[:, 14336:14848]
        stot = AB1f[0:64, 14848:14856]
        seg = AB1f[0:64, 14864:14872]
        lam_a = AB1f[0:64, 14880:14888]
        nA = AB1f[0:64, 14896:14897]
        t2d = AB1f[0:64, 15360:16384]
        b_cols = AFR[:, 5120:6144]

        prog = {k: [] for k in ENGINES}
        PLAN = {k: [] for k in ENGINES}
        nc._plan = PLAN
        C = {s: 0 for s in sem}
        esem = {"tensor": "sT", "vector": "sV", "scalar": "sA", "gpsimd": "sP"}

        def _w(eng, waits):
            for n, v in waits:
                if v > 0:
                    PLAN[eng].append(("w", n, v))
                    prog[eng].append(lambda e, s=sem[n], vv=v: e.wait_ge(s, vv))

        def dma(q, out, in_, wait=(), **kw):
            _w(q, wait)
            sn = "dS" if q == "sync" else "dG"
            C[sn] += 16
            PLAN[q].append(("i", sn, 16))
            prog[q].append(
                lambda e, o=out, i=in_, k2=dict(kw), s=sem[sn]:
                e.dma_start(out=o, in_=i, **k2).then_inc(s, 16))
            return C[sn]

        def dmat(q, out, in_, wait=()):
            _w(q, wait)
            sn = "dS" if q == "sync" else "dG"
            C[sn] += 16
            PLAN[q].append(("i", sn, 16))
            prog[q].append(
                lambda e, o=out, i=in_, s=sem[sn]:
                e.dma_start_transpose(o, i).then_inc(s, 16))
            return C[sn]

        def op(eng, builder, wait=(), inc=True, selfwait=False):
            if selfwait:
                wait = list(wait) + [(esem[eng], C[esem[eng]])]
            _w(eng, wait)
            sn = esem[eng]
            if inc:
                C[sn] += 1
                PLAN[eng].append(("i", sn, 1))
                prog[eng].append(lambda e, b=builder, s=sem[sn]: b(e).then_inc(s, 1))
                return C[sn]
            PLAN[eng].append(("i", None, 0))
            prog[eng].append(lambda e, b=builder: b(e))
            return C[sn]

        def tmark(wait=()):
            return op("tensor", lambda e: e.nop(), wait=wait)

        def mm(out, lhsT, rhs, start, stop, wait=(), inc=False):
            return op("tensor",
                      lambda e, o=out, l=lhsT, r=rhs, st=start, sp=stop:
                      e.matmul(o, l, r, start=st, stop=sp, skip_group_check=True),
                      wait=wait, inc=inc)

        def coll(kind, ins, outs, wait=()):
            _w("gpsimd", wait)
            C["sC"] += 1
            PLAN["gpsimd"].append(("i", "sC", 1))
            prog["gpsimd"].append(
                lambda e, k=kind, i=ins, o=outs, s=sem["sC"]:
                e.collective_compute(k, ALU.bypass, replica_groups=RG, ins=i, outs=o)
                .then_inc(s, 1))
            return C["sC"]

        def barrier():
            snap = [(n, C[n]) for n in C]
            for eng in ENGINES:
                _w(eng, snap)

        try:
            for dst, src in [(c_id, c_id_d), (c_tri, c_tri_d), (c_mup, c_mup_d),
                             (c_sc, c_sc_d), (c_nszp, c_nszp_d), (c_fb, c_fb_d),
                             (c_nw, c_nw_d), (c_lnw, c_lnw_d),
                             (c_cw, c_cw_d), (c_sh, c_sh_d), (c_dtb, c_dtb_d),
                             (c_al, c_al_d), (c_sel, c_sel_d)]:
                dma("gpsimd", dst[:, :], src)
            dma("sync", c_cv[:, :], c_cv_d.partition_broadcast(128))
            dma("sync", Db[:, :], c_df_d.partition_broadcast(128))
            for _i, _v in enumerate([EPS, -0.5, -1.0, 1.0, 1.0 / 4096]):
                nc.const_aps.register(F32, _v, c_cv[:, _i:_i + 1])
            op("vector", lambda e: e.memset(c_ones[:, :], 1.0))
            op("vector", lambda e: e.memset(c_o128[:, :], 1.0))
            op("vector", lambda e: e.memset(c_ones64[:, :], 1.0))
            op("vector", lambda e: e.memset(z64[:, :], 0.0))
            barrier()

            if stop_after is not None and stop_after <= 0:
                raise _StopPlan()
            tdone = [0, 0]
            for k in range(KC):
                b = k % 2
                ld = dma("sync", x2[b], xin[k * 128:(k + 1) * 128, :],
                         wait=[("sT", tdone[b])])
                cv = op("vector", lambda e, b=b: e.tensor_copy(xb2[b][:, :], x2[b]),
                        wait=[("dS", ld), ("sT", tdone[b])])
                sv = op("scalar", lambda e, b=b: e.activation(sq2[b][:, :], x2[b], AF.Square),
                        wait=[("dS", ld), ("sT", tdone[b])])
                for h in range(2):
                    mm(psA[:, h * 512:(h + 1) * 512], c_o128[:, :],
                       xb2[b][:, h * 512:(h + 1) * 512], start=(k == 0), stop=(k == KC - 1),
                       wait=([("sV", cv), ("sA", sv)] if h == 0 else []))
                    tdone[b] = mm(psB[:, h * 512:(h + 1) * 512], c_o128[:, :],
                       sq2[b][:, h * 512:(h + 1) * 512], start=(k == 0), stop=(k == KC - 1),
                       inc=(h == 1))
            tstat = tdone[(KC - 1) % 2]
            v = op("vector", lambda e: e.tensor_copy(b_mr, psA[:, :]),
                   wait=[("sT", tstat)])
            v = op("vector", lambda e: e.tensor_copy(b_rstd, psB[:, :]))
            v = op("vector", lambda e: e.tensor_scalar(out=b_mr, in0=b_mr,
                    scalar1=1.0 / 4096, scalar2=None, op0=ALU.mult))
            v = op("vector", lambda e: e.tensor_scalar(out=b_rstd, in0=b_rstd,
                    scalar1=1.0 / 4096, scalar2=None, op0=ALU.mult))
            v = op("vector", lambda e: e.tensor_mul(x2[0][:, :], b_mr, b_mr))
            v = op("vector", lambda e: e.tensor_sub(b_rstd, b_rstd, x2[0][:, :]))
            a = op("scalar", lambda e: e.activation(b_rstd, b_rstd, AF.Ln, bias=EPS),
                   wait=[("sV", v)])
            a = op("scalar", lambda e: e.activation(b_rstd, b_rstd, AF.Exp, scale=-0.5),
                   selfwait=True)
            v = op("vector", lambda e: e.tensor_mul(b_mr, b_mr, b_rstd), wait=[("sA", a)])
            barrier()

            if stop_after is not None and stop_after <= 1:
                raise _StopPlan()
            tdone = [0, 0]
            for k in range(KC):
                b = k % 2
                ld = dma("sync", x2[b], xin[k * 128:(k + 1) * 128, :],
                         wait=[("sT", tdone[b])])
                v = op("vector", lambda e, b=b: e.tensor_mul(x2[b], x2[b], b_rstd),
                       wait=[("dS", ld)])
                v = op("vector", lambda e, b=b, k=k:
                       e.tensor_sub(xn[:, k * 1024:(k + 1) * 1024], x2[b], b_mr))
                sv = op("scalar", lambda e, b=b, k=k: e.activation(
                    sq2[b][:, :], xn[:, k * 1024:(k + 1) * 1024], AF.Square),
                    wait=[("sV", v), ("sT", tdone[b])])
                for h in range(2):
                    mm(psY[:, h * 512:(h + 1) * 512], c_o128[:, :],
                       sq2[b][:, h * 512:(h + 1) * 512], start=(k == 0), stop=(k == KC - 1),
                       wait=[("sA", sv)] if h == 0 else [])
                    tdone[b] = mm(psY[:, 1024 + h * 512:1024 + (h + 1) * 512], c_o128[:, :],
                       xn[:, k * 1024 + h * 512:k * 1024 + (h + 1) * 512],
                       start=(k == 0), stop=(k == KC - 1),
                       wait=([("sV", v)] if h == 0 else []), inc=(h == 1))
            tstat = tdone[(KC - 1) % 2]
            v2 = op("vector", lambda e: e.tensor_copy(b_rs1, psY[:, 0:1024]),
                    wait=[("sT", tstat)])
            a = op("scalar", lambda e: e.activation(b_rs1, b_rs1, AF.Ln,
                    bias=EPS, scale=1.0 / 4096), wait=[("sV", v2)])
            a = op("scalar", lambda e: e.activation(b_rs1, b_rs1, AF.Exp, scale=-0.5),
                   selfwait=True)
            v = op("vector", lambda e: e.tensor_copy(b_cols, psY[:, 1024:2048]),
                   wait=[("sT", tstat)])
            barrier()

            if stop_after is not None and stop_after <= 2:
                raise _StopPlan()
            qld = [0, 0]
            vext = [0, 0]
            vcp = [0, 0]
            tmm = [0, 0]
            vev = [0, 0]
            fdma = [0, 0]
            for mi in range(KC):
                b = mi % 2
                qld[b] = dma("gpsimd", qx[b], qexp[mi], wait=[("sV", vext[b])])
                vext[b] = op("vector", lambda e, b=b: e.tensor_scalar(
                    out=qx[b], in0=qx[b], scalar1=c_sh[:, 0:1], scalar2=63,
                    op0=ALU.logical_shift_right, op1=ALU.bitwise_and),
                    wait=[("dG", qld[b]), ("sV", vcp[b])])
                vcp[b] = op("vector", lambda e, b=b: e.tensor_copy(vb[b][:, :], qx[b]),
                            wait=[("sT", tmm[b])])
                ps = psA if b == 0 else psB
                for kc in range(KC):
                    for h in range(2):
                        tmm[b] = mm(ps[:, h * 512:(h + 1) * 512],
                           vb[b][:, kc * 128:(kc + 1) * 128],
                           xn[:, kc * 1024 + h * 512:kc * 1024 + (h + 1) * 512],
                           start=(kc == 0), stop=(kc == KC - 1),
                           wait=([("sV", vcp[b]), ("sV", vev[b])] if kc == 0 and h == 0 else []),
                           inc=(kc == KC - 1 and h == 1))
                v = op("vector", lambda e, ps=ps, b=b, mi=mi: e.scalar_tensor_tensor(
                    out=ffn2[b], in0=ps[:, :], scalar=c_sc[:, mi:mi + 1],
                    in1=c_fb[:, mi:mi + 1].broadcast_to([128, 1024]),
                    op0=ALU.mult, op1=ALU.add),
                    wait=[("sT", tmm[b]), ("dS", fdma[b])])
                vev[b] = op("vector", lambda e, b=b, mi=mi: e.scalar_tensor_tensor(
                    out=ffn2[b], in0=b_cols, scalar=c_nszp[:, mi:mi + 1],
                    in1=ffn2[b], op0=ALU.mult, op1=ALU.add))
                fdma[b] = dma("sync", ffn_dr[mi * 128:(mi + 1) * 128, :], ffn2[b],
                              wait=[("sV", vev[b])])
            barrier()

            if stop_after is not None and stop_after <= 3:
                raise _StopPlan()
            for k in range(KC):
                op("vector", lambda e, k=k: e.scalar_tensor_tensor(
                    out=xn[:, k * 1024:(k + 1) * 1024],
                    in0=xn[:, k * 1024:(k + 1) * 1024],
                    scalar=c_lnw[:, k:k + 1], in1=b_rs1, op0=ALU.mult, op1=ALU.mult))
            x1 = xn
            barrier()

            if stop_after is not None and stop_after <= 4:
                raise _StopPlan()
            wld = [0, 0]
            wld4 = [0, 0, 0, 0]
            twf = [0, 0, 0, 0]
            wfree = [0, 0]
            psfree = [0, 0]
            dgld = [0, 0]
            convdone = 0
            last_silu = 0
            silu_x2 = [0, 0]
            xsdma = [0, 0]
            vlast_bc = 0
            for j, mi in enumerate(range(32, 67)):
                b = j % 2
                w4 = j % 4
                wld4[w4] = dma("gpsimd", wb4[w4][:, :], w_in[mi],
                               wait=[("sT", twf[w4])])
                ti = mi - 32
                ps = psA if b == 0 else psB
                for kc in range(KC):
                    for h in range(2):
                        wfree[b] = mm(ps[:, h * 512:(h + 1) * 512],
                           wb4[w4][:, kc * 128:(kc + 1) * 128],
                           x1[:, kc * 1024 + h * 512:kc * 1024 + (h + 1) * 512],
                           start=(kc == 0), stop=(kc == KC - 1),
                           wait=([("dG", wld4[w4]), ("sV", psfree[b])] if kc == 0 and h == 0 else []),
                           inc=(kc == KC - 1 and h == 1))
                twf[w4] = wfree[b]
                if mi < 66:
                    ev = op("vector", lambda e, ps=ps, b=b, mi=mi: e.tensor_tensor(
                        xbc2[b][:, :], ps[:, :],
                        c_mup[:, mi:mi + 1].broadcast_to([128, 1024]), ALU.mult),
                        wait=[("sT", wfree[b])])
                    psfree[b] = ev
                    op("vector", lambda e, b=b, ti=ti: e.tensor_copy(
                        edge[:, ti * 3:(ti + 1) * 3], xbc2[b][:, 0:3]))
                    vlast_bc = op("vector", lambda e, b=b, ti=ti: e.tensor_copy(
                        sendb[:, ti * 3:(ti + 1) * 3], xbc2[b][:, 1021:1024]))
                    dgld[b] = dma("gpsimd", dg2[b][:, :], convd[ti],
                                  wait=[("sT", convdone)])
                    for k in range(4):
                        mm(psY[:, 0:512], dg2[b][:, k * 128:(k + 1) * 128],
                           xbc2[b][:, k:k + 512], start=(k == 0), stop=(k == 3),
                           wait=[("dG", dgld[b]), ("sV", ev), ("sV", last_silu)]
                           if k == 0 else [])
                        convdone = mm(psY[:, 512:1021], dg2[b][:, k * 128:(k + 1) * 128],
                           xbc2[b][:, 512 + k:1021 + k], start=(k == 0), stop=(k == 3),
                           inc=(k == 3))
                    last_silu = op("vector", lambda e, b=b: e.tensor_copy(
                        x2[b][:, 0:1021], psY[:, 0:1021]),
                        wait=[("sT", convdone), ("sA", silu_x2[b])])
                    sv = op("scalar", lambda e, b=b: e.activation(
                        xsst[b][:, 0:1021], x2[b][:, 0:1021], AF.Silu),
                        wait=[("sV", last_silu), ("dS", xsdma[b])])
                    silu_x2[b] = sv
                    if ti < 32:
                        xsdma[b] = dma("sync", xs_plain[ti * 128:(ti + 1) * 128, 3:1024],
                                       xsst[b][:, 0:1021], wait=[("sA", sv)])
                    elif ti == 32:
                        op("vector", lambda e, b=b: e.tensor_copy(
                            B_fm[:, 3:1024], xsst[b][:, 0:1021]), wait=[("sA", sv)])
                    else:
                        op("vector", lambda e, b=b: e.tensor_copy(
                            C_fm[:, 3:1024], xsst[b][:, 0:1021]), wait=[("sA", sv)])
                else:
                    ev = op("vector", lambda e, ps=ps, b=b: e.tensor_tensor(
                        x2[b][:, :], ps[:, :],
                        c_mup[:, 66:67].broadcast_to([128, 1024]), ALU.mult),
                        wait=[("sT", wfree[b])])
                    psfree[b] = ev
                    op("vector", lambda e, b=b: e.tensor_copy(
                        dt_fm, x2[b][0:64, :]))
            d_snd = dma("sync", cc1i.ap(), sendb[:, :], wait=[("sV", vlast_bc)])
            cc1done = coll("AllGather", [cc1i.ap()], [cc1o.ap()], wait=[("dS", d_snd)])

            if stop_after is not None and stop_after <= 5:
                raise _StopPlan()
            zev = [0, 0]
            zfree = [0, 0]
            zdma = [0, 0]
            for mi in range(32):
                b = mi % 2
                w4 = (35 + mi) % 4
                wld4[w4] = dma("gpsimd", wb4[w4][:, :], w_in[mi],
                               wait=[("sT", twf[w4])])
                ps = psA if b == 0 else psB
                for kc in range(KC):
                    for h in range(2):
                        wfree[b] = mm(ps[:, h * 512:(h + 1) * 512],
                           wb4[w4][:, kc * 128:(kc + 1) * 128],
                           x1[:, kc * 1024 + h * 512:kc * 1024 + (h + 1) * 512],
                           start=(kc == 0), stop=(kc == KC - 1),
                           wait=([("dG", wld4[w4]), ("sV", psfree[b]), ("sV", zfree[b])]
                           if kc == 0 and h == 0 else []),
                           inc=(kc == KC - 1 and h == 1))
                twf[w4] = wfree[b]
                psfree[b] = 0
                zfree[b] = op("vector", lambda e, ps=ps, b=b, mi=mi: e.tensor_tensor(
                    x2[b][:, :], ps[:, :],
                    c_mup[:, mi:mi + 1].broadcast_to([128, 1024]), ALU.mult),
                    wait=[("sT", wfree[b]), ("sA", zev[b])])
                zev[b] = op("scalar", lambda e, b=b: e.activation(
                    zst[b][:, :], x2[b][:, :], AF.Silu),
                    wait=[("sV", zfree[b]), ("dS", zdma[b])])
                zdma[b] = dma("sync", z_plain[mi * 128:(mi + 1) * 128, :], zst[b][:, :],
                              wait=[("sA", zev[b])])
                if mi == 2:
                    gld = dma("gpsimd", gth[:, :], cc1o.ap()[0:128, :],
                              wait=[("sC", cc1done)])
                    op("vector", lambda e: e.tensor_scalar(
                        out=gth[:, :], in0=gth[:, :], scalar1=c_sel[:, 0:1],
                        scalar2=None, op0=ALU.mult), wait=[("dG", gld)])
                    edma = 0
                    for ti in range(34):
                        op("vector", lambda e, ti=ti: e.tensor_copy(
                            xpe[:, 0:3], gth[:, ti * 3:(ti + 1) * 3]),
                            wait=[("dS", edma)])
                        op("vector", lambda e, ti=ti: e.tensor_copy(
                            xpe[:, 3:6], edge[:, ti * 3:(ti + 1) * 3]))
                        for o_col, base in ((6, 0), (7, 1), (2, 2)):
                            op("vector", lambda e, ti=ti, o=o_col, base=base:
                               e.tensor_scalar(
                                   out=xpe[:, o:o + 1], in0=xpe[:, base:base + 1],
                                   scalar1=c_cw[:, ti * 4:ti * 4 + 1], scalar2=None,
                                   op0=ALU.mult))
                            for kk in range(1, 4):
                                op("vector", lambda e, ti=ti, o=o_col, base=base, kk=kk:
                                   e.scalar_tensor_tensor(
                                       out=xpe[:, o:o + 1],
                                       in0=xpe[:, base + kk:base + kk + 1],
                                       scalar=c_cw[:, ti * 4 + kk:ti * 4 + kk + 1],
                                       in1=xpe[:, o:o + 1], op0=ALU.mult, op1=ALU.add))
                        v = op("vector", lambda e: e.tensor_copy(xpe[:, 0:2], xpe[:, 6:8]))
                        a = op("scalar", lambda e: e.activation(
                            xpe[:, 3:6], xpe[:, 0:3], AF.Silu), wait=[("sV", v)])
                        if ti < 32:
                            edma = dma("sync", xs_plain[ti * 128:(ti + 1) * 128, 0:3],
                                       xpe[:, 3:6], wait=[("sA", a)])
                        elif ti == 32:
                            op("vector", lambda e: e.tensor_copy(B_fm[:, 0:3], xpe[:, 3:6]),
                               wait=[("sA", a)])
                        else:
                            op("vector", lambda e: e.tensor_copy(C_fm[:, 0:3], xpe[:, 3:6]),
                               wait=[("sA", a)])
            barrier()

            if stop_after is not None and stop_after <= 6:
                raise _StopPlan()
            v = op("vector", lambda e: e.tensor_scalar(out=dtv, in0=dt_fm,
                    scalar1=c_dtb[:, 0:1], scalar2=None, op0=ALU.add))
            v = op("vector", lambda e: e.tensor_scalar(out=t2d, in0=dtv,
                    scalar1=-1.0, scalar2=None, op0=ALU.mult))
            v = op("vector", lambda e: e.tensor_max(t1, dtv, t2d))
            a = op("scalar", lambda e: e.activation(t1, t1, AF.Exp, scale=-1.0),
                   wait=[("sV", v)])
            aT = op("scalar", lambda e: e.activation(t1, t1, AF.Ln, bias=1.0),
                    selfwait=True)
            v = op("vector", lambda e: e.tensor_scalar(out=dt_soft, in0=dtv, scalar1=0.0,
                    scalar2=None, op0=ALU.max))
            vdt = op("vector", lambda e: e.tensor_add(dt_soft, dt_soft, t1),
                     wait=[("sA", aT)])
            aA = op("scalar", lambda e: e.activation(nA, c_al[:, 0:1], AF.Exp),
                    wait=[("sV", vdt)])
            vA = op("vector", lambda e: e.tensor_scalar(out=nA, in0=nA,
                    scalar1=-1.0, scalar2=None, op0=ALU.mult), wait=[("sA", aA)])
            vlog = op("vector", lambda e: e.tensor_scalar(out=logdA, in0=dt_soft,
                    scalar1=nA[:, 0:1], scalar2=None, op0=ALU.mult), wait=[("sV", vA)])
            for c in range(NCH):
                vscan = op("vector", lambda e, c=c: e.tensor_tensor_scan(
                    s_all[:, c * 128:(c + 1) * 128], logdA[:, c * 128:(c + 1) * 128],
                    z64[:, :], 0.0, ALU.add, ALU.add))
            v = op("vector", lambda e: e.tensor_copy(stot, s_all[:, 127:1024:128]),
                   selfwait=True)
            v = op("vector", lambda e: e.tensor_copy(seg[:, 7:8], stot[:, 7:8]),
                   selfwait=True)
            for c in range(6, -1, -1):
                v = op("vector", lambda e, c=c: e.tensor_add(
                    seg[:, c:c + 1], stot[:, c:c + 1], seg[:, c + 1:c + 2]),
                    selfwait=True)
            vseg = v
            op("scalar", lambda e: e.activation(lam_a, stot, AF.Exp), wait=[("sV", vseg)])
            tprev = [0, 0]
            vtr = 0
            for c in range(NCH):
                t = op("tensor", lambda e, c=c: e.transpose(
                    psAv64[0], s_all[:, c * 128:(c + 1) * 128], c_id[0:64, 0:64]),
                    wait=[("sV", max(vscan, tprev[0]))], inc=True)
                tprev[0] = op("vector", lambda e, c=c: e.tensor_copy(
                    s_tok[:, c * 64:(c + 1) * 64], psAv64[0]), wait=[("sT", t)])
                t = op("tensor", lambda e, c=c: e.transpose(
                    psAv64[1], dt_soft[:, c * 128:(c + 1) * 128], c_id[0:64, 0:64]),
                    wait=[("sV", max(vdt, tprev[1]))], inc=True)
                tprev[1] = op("vector", lambda e, c=c: e.tensor_copy(
                    dt_tok[:, c * 64:(c + 1) * 64], psAv64[1]), wait=[("sT", t)])
                vtr = tprev[1]
            d_f = dma("sync", rows_d[0:1, 4096:4608], stot, wait=[("sV", vseg)],
                      allow_slow_non_contiguous=True)
            dma("sync", b_stot, rows_d[0:1, 4096:4608].partition_broadcast(128),
                wait=[("dS", d_f)])
            for c in range(NCH):
                dmat("sync", B_tok[:, c * 128:(c + 1) * 128],
                     B_fm[:, c * 128:(c + 1) * 128])
            barrier()

            if stop_after is not None and stop_after <= 7:
                raise _StopPlan()
            v_bsend = 0
            a_we = 0
            v_wd = 0
            t_wdfree = 0
            for c in range(NCH):
                d_f = dma("sync", rows_d[0:1, 4608 + c * 64:4672 + c * 64],
                          seg[:, c:c + 1], wait=[("sV", v_bsend)],
                          allow_slow_non_contiguous=True)
                d_b = dma("sync", b_send, rows_d[0:1, 4608 + c * 64:4672 + c * 64]
                          .partition_broadcast(128), wait=[("dS", d_f)])
                d_x = dmat("sync", xst[:, :], xs_plain[:, c * 128:(c + 1) * 128],
                           wait=[("sV", v_wd)])
                v_bsend = op("vector", lambda e, c=c: e.tensor_sub(
                    d1, b_send, s_tok[:, c * 64:(c + 1) * 64]),
                    wait=[("dS", d_b), ("sA", a_we)])
                a_we = op("scalar", lambda e: e.activation(we_t, d1, AF.Exp),
                          wait=[("sV", v_bsend)])
                v = op("vector", lambda e, c=c: e.tensor_mul(
                    wdt_b[:, :], we_t, dt_tok[:, c * 64:(c + 1) * 64]),
                    wait=[("sA", a_we)])
                v_wd = op("vector", lambda e: e.tensor_tensor(
                    WD[:, :].rearrange("p (r q) -> p r q", r=64),
                    xst[:, :].rearrange("p (r q) -> p r q", r=64),
                    wdt_b[:, :, None].broadcast_to([128, 64, 64]), ALU.mult),
                    wait=[("sV", v), ("dS", d_x), ("sT", t_wdfree)])
                for q in range(8):
                    if q < 2:
                        o = psA[:, q * 512:(q + 1) * 512]
                    elif q < 4:
                        o = psB[:, (q - 2) * 512:(q - 1) * 512]
                    else:
                        o = psY[:, (q - 4) * 512:(q - 3) * 512]
                    t_wdfree = mm(o, B_tok[:, c * 128:(c + 1) * 128],
                       WD[:, q * 512:(q + 1) * 512],
                       start=(c == 0), stop=(c == NCH - 1),
                       wait=([("sV", v_wd)] if q == 0 else []), inc=(q == 7))
            v = op("vector", lambda e: e.tensor_copy(H_f[:, 0:1024], psA[:, :]),
                   wait=[("sT", t_wdfree)])
            v = op("vector", lambda e: e.tensor_copy(H_f[:, 1024:2048], psB[:, :]))
            vHev = op("vector", lambda e: e.tensor_copy(H_f[:, 2048:4096], psY[:, :]))
            d_h = dma("sync", cc2i.ap(), H_f[:, :], wait=[("sV", vHev)])
            cc2done = coll("AllGather", [cc2i.ap()], [cc2o.ap()], wait=[("dS", d_h)])

            if stop_after is not None and stop_after <= 8:
                raise _StopPlan()
            vprev = vHev
            for c in range(NCH):
                t = op("tensor", lambda e, c=c: e.matmul(
                    psA[:, 0:128], B_fm[:, c * 128:(c + 1) * 128],
                    C_fm[:, c * 128:(c + 1) * 128], start=True, stop=True,
                    skip_group_check=True),
                    wait=[("sV", vprev)], inc=True)
                vprev = op("vector", lambda e, c=c: e.tensor_mul(
                    CBm[:, c * 128:(c + 1) * 128], psA[:, 0:128], c_tri[:, :]),
                    wait=[("sT", t)])
            dma("gpsimd", ind_sb[:, :], c_ind_d)
            gld = dma("gpsimd", srow[:, :], cc2o.ap()[0:128, :], wait=[("sC", cc2done)])
            vHi = op("vector", lambda e: e.tensor_scalar(
                out=H_f[:, :], in0=srow[:, :], scalar1=c_sel[:, 0:1], scalar2=None,
                op0=ALU.mult), wait=[("dG", gld)])
            vHbf = op("vector", lambda e: e.tensor_copy(Hbf[:, :], H_f[:, :]))
            barrier()

            if stop_after is not None and stop_after <= 9:
                raise _StopPlan()
            d_f = dma("sync", srow_d[0:1, :], s_all[0:32, 0:128])
            dsr = dma("sync", srow[:, :], srow_d.partition_broadcast(128),
                      wait=[("dS", d_f)])
            dszt = dmat("sync", szt[:, :], z_plain[:, 0:128])
            dxt = dmat("sync", xst[:, :], xs_plain[:, 0:128])
            dygw = 0
            tlam = 0
            vHq_last = 0
            v_t2 = 0
            v_y1ev = 0
            a_we = 0
            vg = 0
            tY_last = 0
            t_S_last = 0
            for c in range(NCH):
                v_stot = op("vector", lambda e, c=c: e.tensor_sub(
                    d1, b_stot[:, c:512:8], s_tok[:, c * 64:(c + 1) * 64]),
                    wait=[("sA", a_we)])
                a_we = op("scalar", lambda e: e.activation(we_t, d1, AF.Exp),
                          wait=[("sV", v_stot)])
                a_es = op("scalar", lambda e, c=c: e.activation(
                    es_t, s_tok[:, c * 64:(c + 1) * 64], AF.Exp), wait=[("sV", v_t2)])
                for h in range(2):
                    vD = 0
                    for jj in range(32):
                        r = 32 * h + jj
                        vD = op("vector", lambda e, jj=jj, r=r, c=c: e.tensor_scalar(
                            out=srow[:, jj * 128:(jj + 1) * 128],
                            in0=srow[:, jj * 128:(jj + 1) * 128],
                            scalar1=s_tok[:, c * 64 + r:c * 64 + r + 1], scalar2=0.0,
                            op0=ALU.subtract, op1=ALU.min),
                            wait=[("dS", dsr)] if jj == 0 else [])
                    aE = op("scalar", lambda e: e.activation(
                        E_sb[:, :], srow[:, :], AF.Exp),
                        wait=[("sV", vD), ("sT", tY_last), ("dS", dygw)])
                    if h == 0:
                        d_f = dma("sync", srow_d[0:1, :],
                                  s_all[32:64, c * 128:(c + 1) * 128], wait=[("sA", aE)])
                        dsr = dma("sync", srow[:, :], srow_d.partition_broadcast(128),
                                  wait=[("dS", d_f)])
                    elif c < NCH - 1:
                        d_f = dma("sync", srow_d[0:1, :],
                                  s_all[0:32, (c + 1) * 128:(c + 2) * 128],
                                  wait=[("sA", aE)])
                        dsr = dma("sync", srow[:, :], srow_d.partition_broadcast(128),
                                  wait=[("dS", d_f)])
                    vM = op("vector", lambda e, c=c: e.tensor_tensor(
                        E_sb[:, :].rearrange("p (r i) -> p r i", r=32),
                        E_sb[:, :].rearrange("p (r i) -> p r i", r=32),
                        CBm[:, c * 128:(c + 1) * 128][:, None, :]
                        .broadcast_to([128, 32, 128]), ALU.mult), wait=[("sA", aE)])
                    vX = op("vector", lambda e, c=c, h=h: e.tensor_tensor(
                        XD[:, h * 2048:(h + 1) * 2048].rearrange("p (r q) -> p r q", r=32),
                        xst[:, h * 2048:(h + 1) * 2048].rearrange("p (r q) -> p r q", r=32),
                        dt_tok[:, c * 64 + 32 * h:c * 64 + 32 * h + 32][:, :, None]
                        .broadcast_to([128, 32, 64]), ALU.mult),
                        wait=([("dS", dxt)] if h == 0 else []) +
                        [("sV", v_y1ev), ("sT", t_S_last)])
                    for q in range(4):
                        o = psA[:, q * 512:(q + 1) * 512] if q < 2 else                         psB[:, (q - 2) * 512:(q - 1) * 512]
                        mm(o, C_fm[:, c * 128:(c + 1) * 128],
                           Hbf[:, h * 2048 + q * 512:h * 2048 + (q + 1) * 512],
                           start=True, stop=True,
                           wait=[("sV", v_t2), ("sV", vHbf)] if q == 0 else [])
                    for jl in range(32):
                        r = h * 32 + jl
                        tY = mm(psY[:, jl * 64:(jl + 1) * 64],
                           E_sb[:, jl * 128:(jl + 1) * 128],
                           XD[:, r * 64:(r + 1) * 64],
                           start=True, stop=True,
                           wait=([("sV", vM), ("sV", vX), ("sV", v_y1ev),
                                  ("sV", vHq_last)]
                           if jl == 0 else []), inc=(jl == 31))
                    tY_last = tY
                    v_dxs = op("vector", lambda e, h=h: e.tensor_tensor(
                        XW[:, h * 2048:(h + 1) * 2048], xst[:, h * 2048:(h + 1) * 2048],
                        Db[:, h * 2048:(h + 1) * 2048], ALU.mult), wait=[("sT", tY)])
                    v_y1ev = op("vector", lambda e, h=h: e.tensor_add(
                        y_half[:, :], XW[:, h * 2048:(h + 1) * 2048], psY[:, 0:2048]),
                        wait=[("sV", v_dxs)])
                    v = op("vector", lambda e, h=h: e.tensor_tensor(
                        Htmp[:, :], psA[:, :],
                        es_t[:, h * 32:h * 32 + 16][:, :, None]
                        .broadcast_to([128, 16, 64]), ALU.mult),
                        wait=[("sA", a_es), ("sV", v_y1ev)])
                    v = op("vector", lambda e: e.tensor_add(
                        y_half[:, 0:1024], y_half[:, 0:1024], Htmp[:, :]))
                    v = op("vector", lambda e, h=h: e.tensor_tensor(
                        Htmp[:, :], psB[:, :],
                        es_t[:, h * 32 + 16:h * 32 + 32][:, :, None]
                        .broadcast_to([128, 16, 64]), ALU.mult))
                    v_t2 = op("vector", lambda e: e.tensor_add(
                        y_half[:, 1024:2048], y_half[:, 1024:2048], Htmp[:, :]))
                    vg = op("vector", lambda e, h=h: e.tensor_tensor(
                        yg[:, h * 2048:(h + 1) * 2048], y_half[:, :],
                        szt[:, h * 2048:(h + 1) * 2048], ALU.mult),
                        wait=[("dS", dszt)])
                aS = op("scalar", lambda e: e.activation(
                    E_sb[:, :], yg[:, :], AF.Square),
                    wait=[("sV", vg), ("sT", tY_last)])
                v_ms = op("vector", lambda e: e.reduce_sum(
                    ms_col, E_sb[:, :], mybir.AxisListType.X), wait=[("sA", aS)])
                a = op("scalar", lambda e: e.activation(rs_col, ms_col, AF.Ln,
                        bias=EPS, scale=1.0 / 4096), wait=[("sV", v_ms)])
                a_rs = op("scalar", lambda e: e.activation(rs_col, rs_col, AF.Exp,
                        scale=-0.5), selfwait=True)
                v_w = op("vector", lambda e: e.tensor_scalar(
                    out=E_sb[:, :], in0=yg[:, :], scalar1=rs_col, scalar2=None,
                    op0=ALU.mult), wait=[("sA", a_rs)])
                dygw = dma("sync", ygw_dr[c * 128:(c + 1) * 128, :], E_sb[:, :],
                           wait=[("sV", v_w)])
                v_wdt = op("vector", lambda e, c=c: e.tensor_mul(
                    wdt_b[:, :], we_t, dt_tok[:, c * 64:(c + 1) * 64]),
                    wait=[("sA", a_we)])
                vWD = op("vector", lambda e: e.tensor_tensor(
                    WD[:, :].rearrange("p (r q) -> p r q", r=64),
                    xst[:, :].rearrange("p (r q) -> p r q", r=64),
                    wdt_b[:, :, None].broadcast_to([128, 64, 64]), ALU.mult),
                    wait=[("sV", v_wdt), ("sV", v_y1ev)])
                vL = op("vector", lambda e, c=c: e.tensor_scalar(
                    out=lamT[:, :], in0=c_ones64[:, :], scalar1=lam_a[:, c:c + 1],
                    scalar2=None, op0=ALU.mult), wait=[("sT", tlam)])
                vHq = 0
                vt_p = [0, 0]
                vH_p = [0, 0]
                lam_ps = [psB[:, :], psY[:, 0:1024]]
                s_ps = [psA[:, :], psY[:, 1024:2048]]
                for q in range(4):
                    pq = q % 2
                    for hh in range(2):
                        tlam = mm(lam_ps[pq][:, hh * 512:(hh + 1) * 512], lamT[:, :],
                                  ind_sb[:, q * 1024 + hh * 512:q * 1024 + (hh + 1) * 512],
                                  start=True, stop=(hh == 1),
                                  wait=([("sV", vL), ("sV", vt_p[pq])] +
                                        ([("sV", v_t2)] if q < 2 else []))
                                  if hh == 0 else [],
                                  inc=(hh == 1))
                    vt_p[pq] = op("vector", lambda e, q=q, pq=pq: e.tensor_tensor(
                        Htmp[:, :], lam_ps[pq], H_f[:, q * 1024:(q + 1) * 1024],
                        ALU.mult), wait=[("sT", tlam)])
                    for hh in range(2):
                        tS = mm(s_ps[pq][:, hh * 512:(hh + 1) * 512],
                                B_tok[:, c * 128:(c + 1) * 128],
                                WD[:, q * 1024 + hh * 512:q * 1024 + (hh + 1) * 512],
                                start=True, stop=(hh == 1),
                                wait=([("sV", vWD), ("sV", vH_p[pq])] +
                                      ([("sV", v_t2)] if q < 2 else []))
                                if hh == 0 else [],
                                inc=(hh == 1))
                    vHq = op("vector", lambda e, q=q, pq=pq: e.tensor_add(
                        H_f[:, q * 1024:(q + 1) * 1024], Htmp[:, :], s_ps[pq]),
                        wait=[("sT", tS)])
                    vH_p[pq] = vHq
                    vHq_last = vHq
                    t_S_last = tS
                vHbf = op("vector", lambda e: e.tensor_copy(Hbf[:, :], H_f[:, :]),
                          wait=[("sV", vHq)])
                if c < NCH - 1:
                    dxt = dmat("sync", xst[:, :],
                               xs_plain[:, (c + 1) * 128:(c + 2) * 128],
                               wait=[("sV", vWD)])
                    dszt = dmat("sync", szt[:, :],
                                z_plain[:, (c + 1) * 128:(c + 2) * 128],
                                wait=[("sV", vg)])
            barrier()
            if debug:
                dma("sync", d_s, s_all)
                dma("sync", d_dt, dt_fm)
                dma("sync", d_sm[:, 0:8], stot)
                dma("sync", d_sm[:, 8:16], seg)
                dma("sync", d_sm[:, 16:24], lam_a)
                dma("sync", d_sm[:, 24:25], nA, allow_slow_non_contiguous=True)
                dma("sync", d_ds, dt_soft)
                dma("sync", d_ld, logdA)
                dma("sync", d_H, cc2i.ap())
                dma("sync", d_Hg, cc2o.ap())
                dma("sync", d_ygw, ygw_dr)
                dma("sync", d_xs, xs_plain)
                dma("sync", d_z, z_plain)
                dma("sync", d_ffn, ffn_dr)
                barrier()

            if stop_after is not None and stop_after <= 10:
                raise _StopPlan()
            dyl = 0
            for kc in range(KC):
                dyl = dmat("sync", AB1[:, kc * 1024:(kc + 1) * 1024],
                           ygw_dr[:, kc * 128:(kc + 1) * 128])
            wfree = [0, 0]
            vsc = [0, 0]
            vev = [0, 0]
            odma = [0, 0]
            twf = [0, 0, 0, 0]
            wld4 = [0, 0, 0, 0]
            for mi in range(KC):
                b = mi % 2
                w4 = mi % 4
                wld4[w4] = dma("gpsimd", wb4[w4][:, :], w_out[mi],
                               wait=[("sT", twf[w4])])
                dx = dma("sync", x2[b], xin[mi * 128:(mi + 1) * 128, :],
                         wait=[("sV", vev[b])])
                df = dma("sync", ffn2[b], ffn_dr[mi * 128:(mi + 1) * 128, :])
                vsc[b] = op("vector", lambda e, w4=w4: e.tensor_tensor(
                    wb4[w4][:, :].rearrange("p (kc m) -> p kc m", kc=KC),
                    wb4[w4][:, :].rearrange("p (kc m) -> p kc m", kc=KC),
                    c_nw[:, :, None].broadcast_to([128, KC, 128]), ALU.mult),
                    wait=[("dG", wld4[w4])])
                ps = psA if b == 0 else psB
                for kc in range(KC):
                    for th in range(2):
                        wfree[b] = mm(ps[:, th * 512:(th + 1) * 512],
                           wb4[w4][:, kc * 128:(kc + 1) * 128],
                           AB1[:, kc * 1024 + th * 512:kc * 1024 + (th + 1) * 512],
                           start=(kc == 0), stop=(kc == KC - 1),
                           wait=([("sV", vsc[b]), ("dS", dyl), ("sV", vev[b])]
                           if kc == 0 and th == 0 else []),
                           inc=(kc == KC - 1 and th == 1))
                twf[w4] = wfree[b]
                v = op("vector", lambda e, ps=ps, b=b: e.tensor_add(o2[b], ps[:, :], x2[b]),
                       wait=[("sT", wfree[b]), ("dS", df), ("dS", odma[b])])
                vev[b] = op("vector", lambda e, b=b: e.tensor_add(o2[b], o2[b], ffn2[b]))
                odma[b] = dma("sync", out_fm[mi * 128:(mi + 1) * 128, :], o2[b],
                              wait=[("sV", vev[b])])

        except _StopPlan:
            pass
        _w("sync", [("dS", C["dS"])])
        _w("gpsimd", [("dG", C["dG"])])

        block = En(nc.Block())

        @block.sync
        def _(e):
            for f in prog["sync"]:
                f(e)

        @block.scalar
        def _(e):
            for f in prog["scalar"]:
                f(e)

        @block.vector
        def _(e):
            for f in prog["vector"]:
                f(e)

        @block.tensor
        def _(e):
            for f in prog["tensor"]:
                f(e)

        @block.gpsimd
        def _(e):
            for f in prog["gpsimd"]:
                f(e)

    return nc


def _host_prep(inputs):
    f32 = np.float32
    W = np.asarray(inputs["L0_attn_in_proj_w"], f32)
    Wp = np.concatenate([W, np.zeros((4096, 64), f32)], axis=1)
    w_in = np.ascontiguousarray(Wp.reshape(32, 128, MC, 128).transpose(2, 1, 0, 3))
    Wo = np.asarray(inputs["L0_attn_out_proj_w"], f32)
    w_out = np.ascontiguousarray(Wo.reshape(32, 128, 32, 128).transpose(2, 1, 0, 3))
    q = np.asarray(inputs["L0_ffn_qweight"], np.int32)
    k = np.arange(4096)
    src = 3 * (k // 4) + np.maximum(k % 4 - 1, 0)
    qe = np.ascontiguousarray(
        q[:, src].T.reshape(32, 128, 32, 128).transpose(2, 1, 0, 3))
    cw = np.asarray(inputs["L0_attn_conv_w"], f32)
    convd = np.zeros((34, 128, 4, 128), f32)
    ii = np.arange(128)
    for ti in range(34):
        for kk in range(4):
            convd[ti, ii, kk, ii] = cw[ti * 128:(ti + 1) * 128, kk]
    convd = convd.astype(NBF)
    mup = np.asarray(inputs["L0_attn_mup_vector"], f32)
    mupp = np.concatenate([mup, np.zeros(64, f32)])
    c_mup = np.ascontiguousarray(mupp.reshape(MC, 128).T)
    sc = np.asarray(inputs["L0_ffn_weight_scale"], f32).reshape(4096)
    zp = np.asarray(inputs["L0_ffn_weight_zero_point"], f32).reshape(4096)
    fb = np.asarray(inputs["L0_ffn_bias"], f32)
    c_sc = np.ascontiguousarray(sc.reshape(32, 128).T)
    c_nszp = np.ascontiguousarray((-sc * zp).reshape(32, 128).T)
    c_fb = np.ascontiguousarray(fb.reshape(32, 128).T)
    D = np.asarray(inputs["L0_attn_D"], f32)
    c_df = np.repeat(D, 64).reshape(1, 4096).astype(NBF)
    nw = np.asarray(inputs["L0_attn_norm_w"], f32)
    c_nw = np.ascontiguousarray(nw.reshape(32, 128).T)
    lnw = np.asarray(inputs["L0_attn_ln_w"], f32)
    c_lnw = np.ascontiguousarray(lnw.reshape(32, 128).T)
    c_cw = np.zeros((128, 136), f32)
    for ti in range(34):
        c_cw[:, ti * 4:(ti + 1) * 4] = cw[ti * 128:(ti + 1) * 128, :]
    c_sh = ((np.arange(128) % 4 == 1) * 6).astype(np.int32).reshape(128, 1)
    c_dtb = np.asarray(inputs["L0_attn_dt_bias"], f32).reshape(64, 1)
    c_al = np.asarray(inputs["L0_attn_A_log"], f32).reshape(64, 1)
    c_id = np.eye(128, dtype=f32)
    jj = np.arange(128)
    c_tri = (jj[:, None] <= jj[None, :]).astype(f32)
    c_ind = np.repeat(np.eye(64, dtype=f32), 64, axis=1).astype(NBF)

    shared = dict(w_in=w_in, w_out=w_out, qexp=qe, convd=convd, c_id=c_id,
                  c_tri=c_tri, c_ind=c_ind, c_mup=c_mup, c_sc=c_sc,
                  c_nszp=c_nszp, c_fb=c_fb, c_df=c_df, c_nw=c_nw, c_lnw=c_lnw,
                  c_cw=c_cw, c_sh=c_sh, c_dtb=c_dtb, c_al=c_al,
                  c_cv=np.array([[EPS, -0.5, -1.0, 1.0, 1.0 / 4096, 0.0, 0.0, 0.0]],
                                np.float32))
    _ = inputs["L0_attn_residual"]
    x = np.asarray(inputs["L0_attn_hidden_states"], np.float32)
    in_maps = []
    for core in range(8):
        b, h = core // 2, core % 2
        xc = np.ascontiguousarray(x[b, h * 1024:(h + 1) * 1024, :].T)
        sel = np.full((128, 1), 1.0 if h == 1 else 0.0, np.float32)
        m = dict(shared)
        m["xin"] = xc
        m["c_sel"] = sel
        in_maps.append(m)
    return in_maps


def kernel(**inputs):
    if "nc" not in _NC_CACHE:
        _NC_CACHE["nc"] = build_nc()
    nc = _NC_CACHE["nc"]
    in_maps = _host_prep(inputs)
    res = run_bass_kernel_spmd(nc, in_maps, core_ids=list(range(8)))
    out = np.empty((4, 2048, 4096), np.float32)
    for core in range(8):
        b, h = core // 2, core % 2
        out[b, h * 1024:(h + 1) * 1024, :] = res.results[core]["out_fm"].T
    return out
